# version 37
# baseline (speedup 1.0000x reference)
import contextlib
import numpy as np
import ml_dtypes
import concourse.bass as bass
import concourse.mybir as mybir
from concourse.bass_utils import run_bass_kernel_spmd

F32 = mybir.dt.float32
BF16 = mybir.dt.bfloat16
AF = mybir.ActivationFunctionType
ALU = mybir.AluOpType
AX = mybir.AxisListType

S, D = 2048, 1024
NCH = D // 128
NTC = S // 128
NTT = S // 512
CW = 2052
EPS = 1e-5
BASE = 16_416


def _r3(ap, w):
    return ap.rearrange("p (c w) -> p c w", w=w)


def build_program():
    nc = bass.Bass("TRN2", target_bir_lowering=False)

    x0_d = nc.dram_tensor("x0", [D, S], F32, kind="ExternalInput").ap()
    wat_d = nc.dram_tensor("wat", [D, 2 * D], F32, kind="ExternalInput").ap()
    wtr_d = nc.dram_tensor("wtr", [NCH, 128, NCH, 128], F32, kind="ExternalInput").ap()
    wtk_d = nc.dram_tensor("wtk", [NCH, 128, NCH, 128], F32, kind="ExternalInput").ap()
    wtv_d = nc.dram_tensor("wtv", [NCH, 128, NCH, 128], F32, kind="ExternalInput").ap()
    mur_d = nc.dram_tensor("mur", [128, NCH], F32, kind="ExternalInput").ap()
    muk_d = nc.dram_tensor("muk", [128, NCH], F32, kind="ExternalInput").ap()
    ident_d = nc.dram_tensor("identm", [128, 128], F32, kind="ExternalInput").ap()
    mask_d = nc.dram_tensor("maskadd", [128, 1024], BF16, kind="ExternalInput").ap()
    y_d = nc.dram_tensor("y", [D, S], F32, kind="ExternalOutput").ap()
    wvp_d = nc.dram_tensor("wvp", [2 * D, D], BF16, kind="Internal").ap()

    at = nc.alloc_sbuf_tensor_at
    o = BASE
    X0H = at("X0H", [128, NCH * CW], BF16, offset=o)
    X0L = at("X0L", [128, NCH * CW], BF16, offset=o + 32_832)
    KH = at("KH", [128, NCH * 2048], BF16, offset=o + 65_664)
    KL = at("KL", [128, NCH * 2048], BF16, offset=o + 98_432)
    X1F = at("X1F", [128, NCH * 2049], F32, offset=o + 65_664)
    X0F = at("X0F", [128, NCH * 2048], F32, offset=o + 131_200)
    VH = at("VH", [128, NTC * 1024], BF16, offset=o + 131_200)
    VL = at("VL", [128, NTC * 1024], BF16, offset=o + 163_968)
    WQ = at("WQ", [128, NCH * 256], F32, offset=o + 196_736)
    PS = at("PS", [128, 2048], F32, offset=o + 196_736)
    ATOK = at("ATOK", [128, 1024], F32, offset=o + 204_928)
    PT0 = at("PT0", [128, 128], F32, offset=o + 209_024)
    PT1 = at("PT1", [128, 128], F32, offset=o + 209_536)
    MASK = at("MASK", [128, 1024], BF16, offset=o + 210_048)
    IDN = at("IDN", [128, 128], F32, offset=o + 212_096)
    NEGM = at("NEGM", [128, 1], F32, offset=o + 212_608)
    ZCOL = at("ZCOL", [128, 1], F32, offset=o + 212_612, align_bytes=4)
    RZ = at("RZ", [128, 1], F32, offset=o + 212_616, align_bytes=4)
    CONST0 = at("CONST0", [128, 1], F32, offset=o + 212_620, align_bytes=4)
    CONST1 = at("CONST1", [128, 1], F32, offset=o + 212_624, align_bytes=4)
    CONSTB = at("CONSTB", [128, 1], BF16, offset=o + 212_628, align_bytes=4)
    CONSTU = at("CONSTU", [128, 1], mybir.dt.uint8, offset=o + 212_632, align_bytes=4)
    MX4 = at("MX4", [128, 4], F32, offset=o + 212_640, align_bytes=4)
    Z4 = at("Z4", [128, 4], F32, offset=o + 212_656, align_bytes=4)
    LND = at("LND", [128, 1], F32, offset=o + 212_672, align_bytes=4)
    T = [at(f"T{i}", [128, NCH * 512], F32, offset=o + 131_232 + i * 16_384) for i in range(3)]
    T += [at(f"T{i}", [128, NCH * 512], F32, offset=o + (i - 3) * 16_384) for i in range(3, 6)]
    TB = [at(f"TB{i}", [128, NCH * 1024], BF16, offset=o + 131_232 + i * 16_384) for i in range(3)]
    TB += [at(f"TB{i}", [128, NCH * 1024], BF16, offset=o + (i - 3) * 16_384) for i in range(3, 6)]
    WFQE = [at(f"WFQE{h}", [128, 1024], F32, offset=o + 49_152 + h * 4_096) for h in range(2)]
    WFQB = at("WFQB", [128, 4096], BF16, offset=o + 57_344)
    A1R = at("A1R", [1, CW], F32, offset=o + 180_384)
    B1R = at("B1R", [1, CW], F32, offset=o + 188_608)
    SQ8 = at("SQ8", [128, NCH * 256], F32, offset=o + 196_832)
    S1R = at("S1R", [1, 512], F32, offset=o + 196_832)
    S2R = at("S2R", [1, 512], F32, offset=o + 198_880)
    A2R = at("A2R", [1, 512], F32, offset=o + 200_928)
    B2R = at("B2R", [1, 512], F32, offset=o + 202_976)
    MUR = at("MUR", [128, NCH], F32, offset=o + 209_024)
    MUK = at("MUK", [128, NCH], F32, offset=o + 209_056)
    ONESC = at("ONESC", [128, 1], F32, offset=o + 209_088)
    ONESR = at("ONESR", [1, 128], F32, offset=o + 209_120)
    ONESB = at("ONESB", [128, 1], BF16, offset=o + 209_632)

    nc.const_aps.register(F32, 0.0, CONST0.ap())
    nc.const_aps.register(F32, 1.0, CONST1.ap())
    nc.const_aps.register(BF16, 1.0, CONSTB.ap())
    nc.const_aps.register(mybir.dt.uint8, 127, CONSTU.ap())
    LNDV = 0.5 * float(np.log(D))
    nc.const_aps.register(F32, LNDV, LND.ap())

    FSW = at("FSW", [128, 1024], F32, offset=o + 131_200 + 57_344)
    FSH = at("FSH", [128, 1024], BF16, offset=o + 131_200 + 61_440)
    FSL = at("FSL", [128, 1024], BF16, offset=o + 131_200 + 63_488)
    B2B = at("B2B", [128, 512], F32, offset=o + 205_024)
    WQE = at("WQE", [128, 1024], F32, offset=o + 204_928)

    psAB = nc.alloc_psum_tensor("psAB", [128, 2048], F32)
    psE = nc.alloc_psum_tensor("psE", [128, 512], F32)
    psV = nc.alloc_psum_tensor("psV", [128, 1024], F32)
    psT = nc.alloc_psum_tensor("psT", [128, 512], F32)

    XH3 = _r3(X0H.ap(), CW)
    XL3 = _r3(X0L.ap(), CW)
    X0F3 = _r3(X0F.ap(), 2048)
    X1F3 = _r3(X1F.ap(), 2049)
    KH3 = _r3(KH.ap(), 2048)
    KL3 = _r3(KL.ap(), 2048)
    VH3 = _r3(VH.ap(), 1024)
    VL3 = _r3(VL.ap(), 1024)
    PT0B = PT0.ap().bitcast(BF16)
    PT1B = PT1.ap().bitcast(BF16)
    WQ3 = _r3(WQ.ap(), 256)
    WQB4 = WQ.ap().bitcast(BF16).rearrange("p (c s x) -> p c s x", c=NCH, s=2)
    WQE3 = _r3(WQE.ap(), 128)
    wvp4 = wvp_d.rearrange("(c hl p) d -> p c hl d", hl=2, p=128)
    wvp3 = wvp_d.rearrange("(chl p) d -> p chl d", p=128)
    WFQE3 = [_r3(w.ap(), 128) for w in WFQE]
    WFQB4 = WFQB.ap().rearrange("p (c s x) -> p c s x", c=NCH, s=2)
    wfq_ctr = [0]
    T3 = [_r3(t.ap(), 512) for t in T]
    TB3 = [_r3(t.ap(), 1024) for t in TB]
    FSW3 = FSW.ap()
    FSH3 = FSH.ap()
    FSL3 = FSL.ap()

    wat3 = wat_d.rearrange("(c p) d -> p c d", p=128)
    watv3 = wat_d.rearrange("(c p) d -> p c d", p=128)
    wf_src = [wtr_d, wtk_d, wtv_d]
    x0_3 = x0_d.rearrange("(c p) s -> p c s", p=128)
    y_3 = y_d.rearrange("(c p) s -> p c s", p=128)

    stack = contextlib.ExitStack()
    semnames = ("dmx", "dmk", "dmf", "dmo", "dmg", "pe", "dve", "act", "gps")
    sems = {n: stack.enter_context(nc.semaphore(n)) for n in semnames}
    DMAS = ("dmx", "dmk", "dmf", "dmo", "dmg")
    cnt = {k: 0 for k in semnames}
    hw = {}
    ops = {e: [] for e in ("sync", "tensor", "vector", "scalar", "gpsimd")}

    def W(ename, mark):
        if mark is None:
            return
        s, val = mark
        if hw.get((ename, s), 0) >= val:
            return
        hw[(ename, s)] = val
        ops[ename].append(("w", s, val))

    def O(ename, builder):
        ops[ename].append(("o", builder))

    def I(ename, builder, s):
        n = 16 if s in DMAS else 1
        cnt[s] += n
        ops[ename].append(("i", builder, s, n))
        return (s, cnt[s])

    def dmago(stream, out, in_):
        W("sync", (stream, cnt[stream]))
        return I("sync", lambda sy, out=out, in_=in_: sy.dma_start(out=out, in_=in_), stream)

    def guard(ename, builder, s):
        m = I(ename, builder, s)
        W(ename, m)
        return m

    M = {}
    psab_free = {b: None for b in range(4)}
    wf_free = {0: None, 1: None, 2: None}
    wfqb_free = {0: None, 1: None}
    ptb_free = {0: None, 1: None}
    pts_free = {0: None, 1: None}

    dmago("dmx", IDN.ap(), ident_d)
    dmago("dmx", MASK.ap(), mask_d)
    for tt4 in range(4):
        M[("x0s", tt4)] = dmago("dmx" if tt4 % 2 == 0 else "dmk",
                                X0F3[:, :, tt4 * 512:(tt4 + 1) * 512],
                                x0_3[:, :, tt4 * 512:(tt4 + 1) * 512])
    M["const"] = M[("x0s", 2)]
    for tt4 in range(4):
        W("scalar", M[("x0s", tt4)])
        mh = I("scalar", lambda sc, tt4=tt4: sc.copy(
            XH3[:, :, 1 + tt4 * 512: 1 + (tt4 + 1) * 512],
            X0F3[:, :, tt4 * 512:(tt4 + 1) * 512]), "act")
        W("vector", mh)
        M[("x0t", tt4)] = I("vector", lambda v, tt4=tt4: v.scalar_tensor_tensor(
            XL3[:, :, 1 + tt4 * 512: 1 + (tt4 + 1) * 512],
            XH3[:, :, 1 + tt4 * 512: 1 + (tt4 + 1) * 512], -1.0,
            X0F3[:, :, tt4 * 512:(tt4 + 1) * 512], op0=ALU.mult, op1=ALU.add), "dve")
    M["x0"] = M[("x0t", 3)]

    O("vector", lambda v: v.memset(CONST0.ap(), 0.0))
    O("vector", lambda v: v.memset(CONST1.ap(), 1.0))
    O("vector", lambda v: v.memset(CONSTB.ap(), 1.0))
    O("vector", lambda v: v.memset(CONSTU.ap(), 127))
    O("vector", lambda v: v.memset(LND.ap(), LNDV))
    O("vector", lambda v: v.memset(XH3[:, :, 0:1], 0.0))
    M["init"] = I("vector", lambda v: v.memset(XL3[:, :, 0:1], 0.0), "dve")

    def gdma(out, in_):
        W("gpsimd", ("dmg", cnt["dmg"]))
        return I("gpsimd", lambda g, out=out, in_=in_: g.dma_start(out=out, in_=in_), "dmg")

    def split_in(j):
        if j >= NCH:
            return
        if j > 0:
            W("gpsimd", M[("spl_sub", j - 1)])
        else:
            W("gpsimd", M["x0"])
        M[("spl_in", j)] = gdma(FSW3, watv3[:, j, 1024:2048])

    def split_compute(j):
        if j >= NCH:
            return
        W("scalar", M[("spl_in", j)])
        if j > 0:
            W("scalar", M[("spl_out", j - 1)])
        mh = I("scalar", lambda sc: sc.copy(FSH3, FSW3), "act")
        W("gpsimd", mh)
        M[("spl_sub", j)] = I("gpsimd", lambda g: g.tensor_tensor(
            FSL3, FSW3, FSH3, op=ALU.subtract), "gps")
        W("gpsimd", M[("spl_sub", j)])
        gdma(wvp4[:, j, 0, :], FSH3)
        M[("spl_out", j)] = gdma(wvp4[:, j, 1, :], FSL3)
        if j == NCH - 1:
            for qq in range(4):
                M[("wsplitq", 3, qq)] = M[("spl_out", j)]


    M[("kwf", 0)] = dmago("dmk", WQE3[:, :, :], wat3[:, :, 0:128])

    def kw_split(dck):
        s = dck % 2
        W("scalar", M[("kwf", dck)])
        if dck >= 2:
            W("scalar", M[("kmm", dck - 2)])
        mh = I("scalar", lambda sc, s=s: sc.copy(WQB4[:, :, s, 0:128], WQE3[:, :, :]), "act")
        W("vector", mh)
        M[("kw", dck)] = I("vector", lambda v, s=s: v.scalar_tensor_tensor(
            WQB4[:, :, s, 128:256], WQB4[:, :, s, 0:128], -1.0, WQE3[:, :, :],
            op0=ALU.mult, op1=ALU.add), "dve")
        if dck + 1 < NCH:
            W("sync", M[("kw", dck)])
            M[("kwf", dck + 1)] = dmago("dmk", WQE3[:, :, :],
                                        wat3[:, :, (dck + 1) * 128:(dck + 2) * 128])

    kw_split(0)
    for dck in range(NCH):
        s = dck % 2
        W("tensor", M["init"])
        W("tensor", M[("kw", dck)])
        for tt in range(4):
            W("tensor", M[("x0t", tt)])
            W("tensor", psab_free[tt])
            n = 0
            for cc in range(NCH):
                for (wr, xw) in ((0, XH3), (0, XL3), (1, XH3)):
                    n += 1
                    b = (lambda t, tt=tt, cc=cc, s=s, wr=wr, xw=xw,
                         first=(n == 1), lastmm=(n == 24):
                         t.matmul(psAB.ap()[:, tt * 512:(tt + 1) * 512],
                                  WQB4[:, cc, s, wr * 128:(wr + 1) * 128],
                                  xw[:, cc, 1 + tt * 512: 1 + (tt + 1) * 512],
                                  start=first, stop=lastmm))
                    if n == 24:
                        M[("kps", dck, tt)] = I("tensor", b, "pe")
                    else:
                        O("tensor", b)
        M[("kmm", dck)] = M[("kps", dck, 3)]
        if dck + 1 < NCH:
            kw_split(dck + 1)
        for tt in range(4):
            W("vector", M[("kps", dck, tt)])
            guard("vector", lambda v, dck=dck, tt=tt: v.tensor_copy(
                KH3[:, dck, tt * 512:(tt + 1) * 512],
                psAB.ap()[:, tt * 512:(tt + 1) * 512]), "dve")
            psab_free[tt] = I("vector", lambda v, dck=dck, tt=tt: v.scalar_tensor_tensor(
                KL3[:, dck, tt * 512:(tt + 1) * 512],
                KH3[:, dck, tt * 512:(tt + 1) * 512], -1.0,
                psAB.ap()[:, tt * 512:(tt + 1) * 512],
                op0=ALU.mult, op1=ALU.add), "dve")
        split_in(dck)
        split_compute(dck)

    WQ16 = WQ.ap().bitcast(BF16).rearrange("p (chl x) -> p chl x", x=256)
    for q in range(4):
        W("sync", M[("kmm", NCH - 1)] if q == 0 else M[("vmm", q - 1)])
        W("sync", M[("wsplitq", 3, q)])
        M[("vw", q)] = dmago("dmk", WQ16[:, 0:8, :],
                             wvp3[:, 0:8, q * 256:(q + 1) * 256])
        M[("vwb", q)] = dmago("dmx", WQ16[:, 8:16, :],
                              wvp3[:, 8:16, q * 256:(q + 1) * 256])
        for tc in range(NTC):
            bank = tc % 4
            W("tensor", M[("vw", q)])
            W("tensor", M[("vwb", q)])
            W("tensor", M[("x0t", tc // 4)])
            W("tensor", psab_free[bank])
            n = 0
            for cc in range(NCH):
                for (xw, wr) in ((XH3, 0), (XH3, 1), (XL3, 0)):
                    n += 1
                    b = (lambda t, bank=bank, cc=cc, tc=tc, xw=xw, wr=wr,
                         first=(n == 1), lastmm=(n == 24):
                         t.matmul(psAB.ap()[:, bank * 512: bank * 512 + 256],
                                  xw[:, cc, 1 + tc * 128: 1 + (tc + 1) * 128],
                                  WQ16[:, cc * 2 + wr, :],
                                  start=first, stop=lastmm))
                    if n == 24:
                        M[("vps", q, tc)] = I("tensor", b, "pe")
                    else:
                        O("tensor", b)
            W("vector", M[("vps", q, tc)])
            if tc >= 12:
                W("vector", M[("spl_out", NCH - 1)])
            guard("vector", lambda v, tc=tc, q=q, bank=bank: v.tensor_copy(
                VH3[:, tc, q * 256:(q + 1) * 256],
                psAB.ap()[:, bank * 512: bank * 512 + 256]), "dve")
            psab_free[bank] = I("vector",
                                lambda v, tc=tc, q=q, bank=bank: v.scalar_tensor_tensor(
                                    VL3[:, tc, q * 256:(q + 1) * 256],
                                    VH3[:, tc, q * 256:(q + 1) * 256], -1.0,
                                    psAB.ap()[:, bank * 512: bank * 512 + 256],
                                    op0=ALU.mult, op1=ALU.add), "dve")
        M[("vmm", q)] = M[("vps", q, NTC - 1)]

    pending_at = None
    for qc in range(NTC):
        kt_n = qc // 4 + 1
        r = qc % 4
        kw = kt_n * 512
        nkc = qc + 1

        korder = [kt_n - 1] + list(range(kt_n - 1))
        W("tensor", M["const"])
        for ki, kt in enumerate(korder):
            W("tensor", psab_free[kt])
            n = 0
            for dc in range(NCH):
                for (xw, kw3) in ((XH3, KH3), (XH3, KL3), (XL3, KH3)):
                    n += 1
                    b = (lambda t, dc=dc, kt=kt, qc=qc, xw=xw, kw3=kw3,
                         first=(n == 1), lastmm=(n == 24):
                         t.matmul(psAB.ap()[:, kt * 512:(kt + 1) * 512],
                                  xw[:, dc, 1 + qc * 128: 1 + (qc + 1) * 128],
                                  kw3[:, dc, kt * 512:(kt + 1) * 512],
                                  start=first, stop=lastmm))
                    if n == 24:
                        M[("Sb", qc, kt)] = I("tensor", b, "pe")
                    else:
                        O("tensor", b)
            if ki == 0 and pending_at is not None:
                pending_at()
                pending_at = None

        W("vector", M["const"])
        for kt in korder:
            W("vector", M[("Sb", qc, kt)])
            if kt == kt_n - 1:
                guard("vector", (lambda v, kt_n=kt_n, r=r: v.tensor_tensor(
                    psAB.ap()[:, (kt_n - 1) * 512: kt_n * 512],
                    psAB.ap()[:, (kt_n - 1) * 512: kt_n * 512],
                    MASK.ap()[:, 384 - 128 * r: 896 - 128 * r], op=ALU.add)), "dve")
            b = (lambda v, kt=kt: v.reduce_max(
                MX4.ap()[:, kt:kt + 1], psAB.ap()[:, kt * 512:(kt + 1) * 512],
                axis=AX.X))
            guard("vector", b, "dve")
        M[("negm", qc)] = I("vector", lambda v, kt_n=kt_n: v.reduce_max(
            NEGM.ap(), MX4.ap()[:, 0:kt_n], axis=AX.X, negate=True), "dve")

        W("scalar", M["init"])
        W("scalar", M[("negm", qc)])
        for kt in range(kt_n):
            if qc > 0:
                kcp = min(4 * kt + 3, qc - 1)
                W("scalar", M[("ptt", qc - 1, kcp)])
            M[("expb", qc, kt)] = I("scalar", lambda sc, kt=kt: sc.activation(
                out=PS.ap()[:, kt * 512:(kt + 1) * 512],
                in_=psAB.ap()[:, kt * 512:(kt + 1) * 512],
                func=AF.Exp, bias=NEGM.ap(), scale=1.0,
                accum_out=Z4.ap()[:, kt:kt + 1]), "act")
            psab_free[kt] = M[("expb", qc, kt)]
        M[("exp", qc)] = M[("expb", qc, kt_n - 1)]

        W("vector", M[("exp", qc)])
        W("vector", M.get(("aev", qc - 1)))
        guard("vector", lambda v, kt_n=kt_n: v.reduce_sum(
            ZCOL.ap(), Z4.ap()[:, 0:kt_n], axis=AX.X), "dve")
        M[("rz", qc)] = I("vector", lambda v: v.reciprocal(RZ.ap(), ZCOL.ap()), "dve")

        def emit_T(kc):
            par = kc % 2
            ptb = psE if par == 0 else psT
            W("tensor", M[("expb", qc, kc // 4)])
            W("tensor", ptb_free[par])
            M[("ptt", qc, kc)] = I("tensor", lambda t, kc=kc, ptb=ptb: t.transpose(
                ptb.ap()[:, 0:128], PS.ap()[:, kc * 128:(kc + 1) * 128], IDN.ap()), "pe")

        emit_T(0)
        for kc in range(nkc):
            par = kc % 2
            ptb = psE if par == 0 else psT
            pts = PT0 if par == 0 else PT1

            ptsb = PT0B if par == 0 else PT1B
            W("vector", M[("ptt", qc, kc)])
            W("vector", pts_free[par])
            guard("vector", lambda v, ptb=ptb, ptsb=ptsb:
                  v.tensor_copy(ptsb[:, 0:128], ptb.ap()[:, 0:128]), "dve")
            ptb_free[par] = M[("ptev", qc, kc)] = I(
                "vector", lambda v, ptb=ptb, ptsb=ptsb: v.scalar_tensor_tensor(
                    ptsb[:, 128:256], ptsb[:, 0:128], -1.0, ptb.ap()[:, 0:128],
                    op0=ALU.mult, op1=ALU.add), "dve")

            if kc + 1 < nkc:
                emit_T(kc + 1)

            W("tensor", M[("ptev", qc, kc)])
            if kc == 0:
                W("tensor", M.get(("aev", qc - 1)))
            first, last = kc == 0, kc == nkc - 1
            for n6, (pofs, vw, dh) in enumerate((
                    (0, VH3, 0), (0, VH3, 1), (0, VL3, 0), (0, VL3, 1),
                    (128, VH3, 0), (128, VH3, 1))):
                b = (lambda t, kc=kc, ptsb=ptsb, pofs=pofs, vw=vw, dh=dh,
                     first=first, last=last, n6=n6: t.matmul(
                         psV.ap()[:, dh * 512:(dh + 1) * 512],
                         ptsb[:, pofs:pofs + 128], vw[:, kc, dh * 512:(dh + 1) * 512],
                         start=(first and n6 < 2), stop=(last and n6 >= 4)))
                if n6 == 5:
                    mark = I("tensor", b, "pe")
                elif last and n6 == 4:
                    M[("pvh", qc)] = I("tensor", b, "pe")
                else:
                    O("tensor", b)
            pts_free[par] = mark
            if kc == nkc - 1:
                M[("pv", qc)] = mark

        W("vector", M[("pvh", qc)])
        W("vector", M[("rz", qc)])
        M[("aevh", qc)] = I("vector", lambda v: v.tensor_scalar_mul(
            ATOK.ap()[:, 0:512], psV.ap()[:, 0:512], RZ.ap()), "dve")
        W("vector", M[("pv", qc)])
        M[("aev", qc)] = I("vector", lambda v: v.tensor_scalar_mul(
            ATOK.ap()[:, 512:1024], psV.ap()[:, 512:1024], RZ.ap()), "dve")

        def emit_at_cmb(qc=qc):
            W("tensor", M[("aevh", qc)])
            W("tensor", ptb_free[0])
            W("tensor", ptb_free[1])
            for dc in range(NCH):
                if dc == 4:
                    W("tensor", M[("aev", qc)])
                tgt = psE if dc < 4 else psT
                b = (lambda t, dc=dc, tgt=tgt: t.transpose(
                    tgt.ap()[:, (dc % 4) * 128: (dc % 4 + 1) * 128],
                    ATOK.ap()[:, dc * 128:(dc + 1) * 128], IDN.ap()))
                if dc == NCH - 1:
                    M[("at", qc)] = I("tensor", b, "pe")
                else:
                    O("tensor", b)

            W("vector", M[("at", qc)])
            W("vector", M["x0"])
            atok3 = _r3(ATOK.ap(), 128)
            pse3 = _r3(psE.ap(), 128)
            pst3 = _r3(psT.ap(), 128)
            guard("vector", lambda v, qc=qc: v.tensor_tensor(
                atok3, XH3[:, :, 1 + qc * 128: 1 + (qc + 1) * 128],
                XL3[:, :, 1 + qc * 128: 1 + (qc + 1) * 128], op=ALU.add), "dve")
            guard("vector", lambda v: v.scalar_tensor_tensor(
                _r3(ATOK.ap()[:, 0:512], 128), _r3(ATOK.ap()[:, 0:512], 128), 2.0,
                pse3, op0=ALU.mult, op1=ALU.add), "dve")
            mA = guard("vector", lambda v: v.scalar_tensor_tensor(
                _r3(ATOK.ap()[:, 512:1024], 128), _r3(ATOK.ap()[:, 512:1024], 128),
                2.0, pst3, op0=ALU.mult, op1=ALU.add), "dve")
            ptb_free[0] = ptb_free[1] = mA
            guard("vector", lambda v, qc=qc: v.tensor_copy(
                XH3[:, :, 1 + qc * 128: 1 + (qc + 1) * 128], atok3), "dve")
            M[("cmb", qc)] = I("vector", lambda v, qc=qc: v.scalar_tensor_tensor(
                XL3[:, :, 1 + qc * 128: 1 + (qc + 1) * 128],
                XH3[:, :, 1 + qc * 128: 1 + (qc + 1) * 128], -1.0, atok3,
                op0=ALU.mult, op1=ALU.add), "dve")

        if qc + 1 < NTC and (qc + 1) // 4 + 1 >= 3:
            pending_at = emit_at_cmb
        else:
            emit_at_cmb()
    psab_free[2] = psab_free[3] = M[("cmb", NTC - 1)]
    psab_free[0] = psab_free[1] = M[("cmb", NTC - 1)]

    W("sync", M[("at", NTC - 1)])
    dmago("dmx", MUR.ap(), mur_d)
    M["mu"] = dmago("dmx", MUK.ap(), muk_d)

    W("vector", M[("cmb", NTC - 1)])
    O("vector", lambda v: v.memset(ONESR.ap(), 1.0))
    O("vector", lambda v: v.memset(ONESB.ap(), 1.0))
    M["pc0"] = I("vector", lambda v: v.memset(ONESC.ap(), 1.0), "dve")

    SQ83 = _r3(SQ8.ap(), 256)
    O("vector", lambda v: v.memset(X1F3[:, :, 0:1], 0.0))
    for tt in range(NTT):
        W("vector", M[("cmb", NTC - 1)])
        W("vector", M["pc0"])
        M[("rec", tt)] = guard("vector", lambda v, tt=tt: v.tensor_tensor(
            X1F3[:, :, 1 + tt * 512: 1 + (tt + 1) * 512],
            XH3[:, :, 1 + tt * 512: 1 + (tt + 1) * 512],
            XL3[:, :, 1 + tt * 512: 1 + (tt + 1) * 512], op=ALU.add), "dve")
        W("tensor", M[("rec", tt)])
        W("tensor", M["pc0"])
        W("tensor", psab_free[tt])
        for cc in range(NCH):
            b = (lambda t, cc=cc, tt=tt, first=(cc == 0), lastmm=(cc == NCH - 1):
                 t.matmul(psAB.ap()[0:1, tt * 512:(tt + 1) * 512], ONESC.ap(),
                          X1F3[:, cc, 1 + tt * 512: 1 + (tt + 1) * 512],
                          start=first, stop=lastmm))
            if cc == NCH - 1:
                M[("s1", tt)] = I("tensor", b, "pe")
            else:
                O("tensor", b)
        s2bank = (psE.ap() if tt == 0 else
                  psV.ap()[:, 0:512] if tt == 1 else
                  psV.ap()[:, 512:1024] if tt == 2 else psT.ap())
        for hf in range(2):
            ix = tt * 2 + hf
            s2ap = s2bank[0:1, hf * 256:(hf + 1) * 256]
            sqb = SQ83 if ix % 2 == 0 else _r3(T[1].ap()[:, 0:2048], 256)
            W("scalar", M[("rec", tt)])
            if ix >= 2:
                W("scalar", M[("s2h", (ix - 2) // 2, (ix - 2) % 2)])
            msq = I("scalar", lambda sc, tt=tt, hf=hf, sqb=sqb: sc.activation(
                out=sqb[:, :, :],
                in_=X1F3[:, :, 1 + tt * 512 + hf * 256: 1 + tt * 512 + (hf + 1) * 256],
                func=AF.Square), "act")
            W("tensor", msq)
            for cc in range(NCH):
                last = cc == NCH - 1
                b = (lambda t, cc=cc, s2ap=s2ap, sqb=sqb: t.matmul(
                    s2ap, ONESC.ap(), sqb[:, cc, :],
                    start=(cc == 0), stop=(cc == NCH - 1)))
                if last:
                    M[("s2h", tt, hf)] = I("tensor", b, "pe")
                else:
                    O("tensor", b)
        W("vector", M[("s2h", tt, 1)])
        M[("s2ev", tt)] = I("vector", lambda v, tt=tt, s2bank=s2bank: v.tensor_copy(
            B1R.ap()[0:1, 1 + tt * 512: 1 + (tt + 1) * 512],
            s2bank[0:1, 0:512]), "dve")

        W("vector", M[("s1", tt)])
        psab_free[tt] = M[("s1ev", tt)] = I("vector", lambda v, tt=tt: v.tensor_copy(
            A1R.ap()[0:1, 1 + tt * 512: 1 + (tt + 1) * 512],
            psAB.ap()[0:1, tt * 512:(tt + 1) * 512]), "dve")

    r1 = SQ8.ap()[0:1, 0:2048]
    a1 = A1R.ap()[0:1, 1:2049]
    b1 = B1R.ap()[0:1, 1:2049]

    for _tt in range(NTT):
        W("vector", M[("s1ev", _tt)])
        W("vector", M[("s2ev", _tt)])
    O("vector", lambda v: v.tensor_scalar_mul(a1, a1, 1.0 / D))
    guard("vector", lambda v: v.tensor_scalar_mul(b1, b1, 1.0 / D), "dve")
    guard("vector", lambda v: v.tensor_tensor(r1, a1, a1, op=ALU.mult), "dve")
    guard("vector", lambda v: v.tensor_tensor(b1, b1, r1, op=ALU.subtract), "dve")
    M["ln1v"] = I("vector", lambda v: v.tensor_scalar_add(b1, b1, EPS), "dve")

    W("scalar", M["ln1v"])
    guard("scalar", lambda sc: sc.activation(out=r1, in_=b1, func=AF.Ln), "act")
    M["ln1a"] = I("scalar", lambda sc: sc.activation(out=r1, in_=r1, func=AF.Exp, scale=-0.5), "act")

    W("vector", M["ln1a"])
    guard("vector", lambda v: v.scalar_tensor_tensor(b1, a1, -1.0, r1, op0=ALU.mult, op1=ALU.mult), "dve")
    O("vector", lambda v: v.tensor_copy(a1, r1))
    O("vector", lambda v: v.memset(A1R.ap()[0:1, 0:1], 0.0))
    O("vector", lambda v: v.memset(A1R.ap()[0:1, 2049:CW], 0.0))
    O("vector", lambda v: v.memset(B1R.ap()[0:1, 0:1], 0.0))
    M["ln1"] = I("vector", lambda v: v.memset(B1R.ap()[0:1, 2049:CW], 0.0), "dve")

    def x1n_bc(t4):
        ts4 = t4 * 512
        pea = psE.ap() if t4 % 2 == 0 else psV.ap()[:, 512:1024]
        peb = psV.ap()[:, 0:512] if t4 % 2 == 0 else psT.ap()
        W("tensor", M["ln1"])
        if t4 >= 2:
            W("tensor", M[("x1n", t4 - 2)])
        O("tensor", lambda t, ts4=ts4, pea=pea: t.matmul(
            pea, ONESR.ap(), A1R.ap()[0:1, 1 + ts4: 513 + ts4], start=True, stop=True))
        M[("x1bc", t4)] = I("tensor", lambda t, ts4=ts4, peb=peb: t.matmul(
            peb, ONESR.ap(), B1R.ap()[0:1, 1 + ts4: 513 + ts4], start=True, stop=True), "pe")

    def x1n_cc(t4, cc):
        ts4 = t4 * 512
        pea = psE.ap() if t4 % 2 == 0 else psV.ap()[:, 512:1024]
        peb = psV.ap()[:, 0:512] if t4 % 2 == 0 else psT.ap()
        if cc == 0:
            W("vector", M[("x1bc", t4)])
        guard("vector", lambda v, cc=cc, ts4=ts4, pea=pea: v.tensor_tensor(
            X1F3[:, cc, 1 + ts4: 513 + ts4], X1F3[:, cc, 1 + ts4: 513 + ts4],
            pea, op=ALU.mult), "dve")
        b = (lambda v, cc=cc, ts4=ts4, peb=peb: v.tensor_tensor(
            X1F3[:, cc, 1 + ts4: 513 + ts4], X1F3[:, cc, 1 + ts4: 513 + ts4],
            peb, op=ALU.add))
        if cc == NCH - 1:
            M[("x1n", t4)] = guard("vector", b, "dve")
        else:
            O("vector", b)

    x1n_bc(0)
    for _cc in range(NCH):
        x1n_cc(0, _cc)

    def emit_prep_head(u):
        ts = u * 512
        W("vector", M[("x1n", u)])
        if u > 0:
            W("vector", M[("x1n", u - 1)])
        W("vector", M["mu"])
        if u > 0:
            W("vector", M[("mm", "v", NCH - 1, u - 1)])
            W("vector", M[("x2p", u - 1)])
        M[("d", u)] = guard("vector", lambda v, ts=ts: v.tensor_tensor(
            T3[2][:, :, :], X1F3[:, :, 1 + ts: 513 + ts], X1F3[:, :, ts: 512 + ts],
            op=ALU.subtract), "dve")
        for cc in range(NCH):
            b = (lambda v, cc=cc, ts=ts: v.scalar_tensor_tensor(
                T3[5][:, cc, :], T3[2][:, cc, :], MUR.ap()[:, cc:cc + 1],
                X1F3[:, cc, ts: 512 + ts], op0=ALU.mult, op1=ALU.add))
            if cc == NCH - 1:
                guard("vector", b, "dve")
            else:
                O("vector", b)
        guard("vector", lambda v: v.tensor_copy(
            TB3[3][:, :, 0:512], T3[5][:, :, :]), "dve")
        M[("xr", u)] = guard("vector", lambda v: v.scalar_tensor_tensor(
            TB3[3][:, :, 512:1024], TB3[3][:, :, 0:512],
            -1.0, T3[5][:, :, :], op0=ALU.mult, op1=ALU.add), "dve")

    def tail_piece_g(u, k):
        ts2 = u * 512
        if k == 0:
            W("vector", M[("xr", u)])
        if k < NCH:
            b = (lambda v, cc=k, ts2=ts2: v.scalar_tensor_tensor(
                T3[5][:, cc, :], T3[2][:, cc, :], MUK.ap()[:, cc:cc + 1],
                X1F3[:, cc, ts2: 512 + ts2], op0=ALU.mult, op1=ALU.add))
            if k == NCH - 1:
                guard("vector", b, "dve")
            else:
                O("vector", b)
        elif k == NCH:
            guard("vector", lambda v: v.tensor_copy(
                TB3[4][:, :, 0:512], T3[5][:, :, :]), "dve")
        else:
            M[("xk", u)] = guard("vector", lambda v: v.scalar_tensor_tensor(
                TB3[4][:, :, 512:1024], TB3[4][:, :, 0:512],
                -1.0, T3[5][:, :, :], op0=ALU.mult, op1=ALU.add), "dve")

    emit_prep_head(0)
    drip = []
    pe_drip = []
    for _k in range(NCH + 2):
        drip.append(lambda _k=_k: tail_piece_g(0, _k))
    for _t4 in range(1, NTT):
        drip.append(lambda _t4=_t4: x1n_bc(_t4))
        for _cc in range(NCH):
            drip.append(lambda _t4=_t4, _cc=_cc: x1n_cc(_t4, _cc))

    for tt in range(NTT):
        ts = tt * 512

        def wfq_load(name, wi, dck, tt):
            h = wfq_ctr[0] % 2
            wfq_ctr[0] += 1
            W("sync", wf_free[h])
            W("sync", M[("rec", NTT - 1)])
            M[("wfq", name, dck, tt)] = dmago(
                "dmf" if h == 0 else "dmx", WFQE3[h][:, :, :],
                wf_src[wi][dck, :, :, :])
            M[("wfqh", name, dck, tt)] = h

        def wfq_split(name, dck, tt):
            s = dck % 2
            h = M[("wfqh", name, dck, tt)]
            W("scalar", M[("wfq", name, dck, tt)])
            W("scalar", wfqb_free[s])
            mh = I("scalar", lambda sc, s=s, h=h: sc.copy(
                WFQB4[:, :, s, 0:128], WFQE3[h][:, :, :]), "act")
            W("vector", mh)
            wf_free[h] = M[("wfl", name, dck, tt)] = I(
                "vector", lambda v, s=s, h=h: v.scalar_tensor_tensor(
                    WFQB4[:, :, s, 128:256], WFQB4[:, :, s, 0:128], -1.0,
                    WFQE3[h][:, :, :], op0=ALU.mult, op1=ALU.add), "dve")

        def emit_chain(name, wi, rhs_tb, rhs_mark, evac_fn, nxt, tt=tt):
            for d0 in (0, 1):
                if ("wfq", name, d0, tt) not in M:
                    wfq_load(name, wi, d0, tt)
            if ("wfl", name, 0, tt) not in M:
                wfq_split(name, 0, tt)
            for dck in range(NCH):
                bank = dck % 4
                s = dck % 2
                W("tensor", M[("wfl", name, dck, tt)])
                W("tensor", rhs_mark)
                W("tensor", psab_free[bank])
                n = 0
                for cc in range(NCH):
                    for (wrow, xofs) in ((0, 0), (0, 512), (1, 0)):
                        n += 1
                        lastmm = n == 24
                        b = (lambda t, cc=cc, s=s, bank=bank, wrow=wrow,
                             xofs=xofs, rhs_tb=rhs_tb, first=(n == 1), lastmm=lastmm:
                             t.matmul(psAB.ap()[:, bank * 512:(bank + 1) * 512],
                                      WFQB4[:, cc, s, wrow * 128:(wrow + 1) * 128],
                                      TB3[rhs_tb][:, cc, xofs: xofs + 512],
                                      start=first, stop=lastmm))
                        if lastmm:
                            M[("mm", name, dck, tt)] = I("tensor", b, "pe")
                        else:
                            O("tensor", b)
                wfqb_free[s] = M[("mm", name, dck, tt)]
                if dck + 1 < NCH and ("wfl", name, dck + 1, tt) not in M:
                    wfq_split(name, dck + 1, tt)
                if dck + 2 < NCH:
                    wfq_load(name, wi, dck + 2, tt)
                if nxt is not None and dck in (5, 6):
                    wfq_load(nxt[0], nxt[1], dck - 5, nxt[2])
                if nxt is not None and dck in (6, 7):
                    wfq_split(nxt[0], dck - 6, nxt[2])
                evac_fn(dck)
                if pe_drip and dck == 2:
                    pe_drip.pop(0)()
                for _ in range(3 if name == "r" else 2 if name == "k" else 0):
                    if drip:
                        drip.pop(0)()

        def ev_r(dc, tt=tt):
            W("scalar", M[("mm", "r", dc, tt)])
            psab_free[dc % 4] = M[("ev", "r", dc, tt)] = I("scalar", lambda sc, dc=dc: sc.activation(
                out=T3[1][:, dc, :], in_=psAB.ap()[:, (dc % 4) * 512:(dc % 4 + 1) * 512],
                func=AF.Sigmoid), "act")

        emit_chain("r", 0, 3, M[("xr", tt)], ev_r, ("k", 1, tt))

        def ev_k(dc, tt=tt):
            W("scalar", M[("mm", "k", dc, tt)])
            psab_free[dc % 4] = M[("ev", "k", dc, tt)] = I("scalar", lambda sc, dc=dc: sc.activation(
                out=T3[2][:, dc, :], in_=psAB.ap()[:, (dc % 4) * 512:(dc % 4 + 1) * 512],
                func=AF.Relu), "act")
            if dc in (3, 7):
                hf = dc // 4
                c0, c1 = hf * 4, hf * 4 + 4
                W("scalar", M[("ev", "k", dc, tt)])
                mkf = I("scalar", lambda sc, c0=c0, c1=c1: sc.activation(
                    out=T3[5][:, c0:c1, :], in_=T3[2][:, c0:c1, :], func=AF.Square), "act")
                W("vector", mkf)
                if hf == 0:
                    W("vector", M[("mm", "r", NCH - 1, tt)])
                guard("vector", lambda v, c0=c0, c1=c1: v.tensor_copy(
                    TB3[3][:, c0:c1, 0:512], T3[5][:, c0:c1, :]), "dve")
                m = guard("vector", lambda v, c0=c0, c1=c1: v.scalar_tensor_tensor(
                    TB3[3][:, c0:c1, 512:1024], TB3[3][:, c0:c1, 0:512], -1.0,
                    T3[5][:, c0:c1, :], op0=ALU.mult, op1=ALU.add), "dve")
                if hf == 1:
                    M[("ka", tt)] = m

        emit_chain("k", 1, 4, M[("xk", tt)], ev_k, ("v", 2, tt))

        def ev_v(dc, tt=tt):
            W("vector", M[("mm", "v", dc, tt)])
            guard("vector", lambda v, dc=dc: v.tensor_tensor(
                T3[2][:, dc, :], psAB.ap()[:, (dc % 4) * 512:(dc % 4 + 1) * 512],
                T3[1][:, dc, :], op=ALU.mult), "dve")
            psab_free[dc % 4] = M[("ev", "v", dc, tt)] = I("vector", lambda v, dc=dc, ts=ts: v.tensor_tensor(
                T3[0][:, dc, :], X1F3[:, dc, 1 + ts: 513 + ts], T3[2][:, dc, :], op=ALU.add), "dve")
            if dc in (3, 7) and tt + 1 < NTT:
                u = tt + 1
                ts2 = u * 512
                hf = dc // 4
                c0, c1 = hf * 4, hf * 4 + 4
                W("vector", M[("x1n", u)])
                W("vector", M[("x1n", tt)])
                W("vector", M["mu"])
                W("vector", M[("ev", "v", dc, tt)])
                m_d = guard("vector", lambda v, c0=c0, c1=c1, ts2=ts2: v.tensor_tensor(
                    T3[2][:, c0:c1, :], X1F3[:, c0:c1, 1 + ts2: 513 + ts2],
                    X1F3[:, c0:c1, ts2: 512 + ts2], op=ALU.subtract), "dve")
                if hf == 1:
                    M[("d", u)] = m_d
                for cc in range(c0, c1):
                    b = (lambda v, cc=cc, ts2=ts2: v.scalar_tensor_tensor(
                        T3[5][:, cc, :], T3[2][:, cc, :], MUR.ap()[:, cc:cc + 1],
                        X1F3[:, cc, ts2: 512 + ts2], op0=ALU.mult, op1=ALU.add))
                    if cc == c1 - 1:
                        guard("vector", b, "dve")
                    else:
                        O("vector", b)
                if hf == 1:
                    W("vector", M[("mm", "v", NCH - 1, tt)])
                    guard("vector", lambda v: v.tensor_copy(
                        TB3[3][:, :, 0:512], T3[5][:, :, :]), "dve")
                    M[("xr", u)] = guard("vector", lambda v: v.scalar_tensor_tensor(
                        TB3[3][:, :, 512:1024], TB3[3][:, :, 0:512],
                        -1.0, T3[5][:, :, :], op0=ALU.mult, op1=ALU.add), "dve")

        emit_chain("v", 2, 3, M[("ka", tt)], ev_v,
                   ("r", 0, tt + 1) if tt + 1 < NTT else None)
        M[("x2p", tt)] = M[("ev", "v", NCH - 1, tt)]

        for hf in range(2):
            c0, c1 = hf * 4, hf * 4 + 4
            W("scalar", M[("ev", "v", c1 - 1, tt)])
            M[("sq2", tt, hf)] = I("scalar", lambda sc, c0=c0, c1=c1: sc.activation(
                out=T3[1][:, c0:c1, :], in_=T3[0][:, c0:c1, :], func=AF.Square), "act")

        if tt > 0:
            W("tensor", M[("y", tt - 1)])
        W("tensor", M[("x1n", NTT - 1)])
        for cc in range(NCH):
            last = cc == NCH - 1
            W("tensor", M[("ev", "v", cc, tt)])
            b = (lambda t, cc=cc: t.matmul(psE.ap()[0:1, 0:512], ONESC.ap(), T3[0][:, cc, :],
                                           start=(cc == 0), stop=(cc == NCH - 1)))
            if last:
                M[("l2s1", tt)] = I("tensor", b, "pe")
            else:
                O("tensor", b)
        for cc in range(NCH):
            last = cc == NCH - 1
            W("tensor", M[("sq2", tt, cc // 4)])
            b = (lambda t, cc=cc: t.matmul(psV.ap()[0:1, 0:512], ONESC.ap(), T3[1][:, cc, :],
                                           start=(cc == 0), stop=(cc == NCH - 1)))
            if last:
                M[("l2s2", tt)] = I("tensor", b, "pe")
            else:
                O("tensor", b)

        W("vector", M[("l2s2", tt)])
        s1 = S1R.ap()[0:1, :]
        s2 = S2R.ap()[0:1, :]
        guard("vector", lambda v: v.tensor_copy(s1, psE.ap()[0:1, 0:512]), "dve")
        guard("vector", lambda v: v.tensor_tensor(A2R.ap()[0:1, :], s1, s1, op=ALU.mult), "dve")
        guard("vector", lambda v: v.scalar_tensor_tensor(
            s2, A2R.ap()[0:1, :], -1.0 / D, psV.ap()[0:1, 0:512],
            op0=ALU.mult, op1=ALU.add), "dve")
        M[("l2v", tt)] = I("vector", lambda v: v.tensor_scalar_add(s2, s2, D * EPS), "dve")

        W("scalar", M[("l2v", tt)])
        guard("scalar", lambda sc: sc.activation(out=A2R.ap()[0:1, :], in_=S2R.ap()[0:1, :],
                                                 func=AF.Ln), "act")
        M[("l2a", tt)] = I("scalar", lambda sc: sc.activation(
            out=A2R.ap()[0:1, :], in_=A2R.ap()[0:1, :], func=AF.Exp, scale=-0.5,
            bias=LNDV), "act")

        W("vector", M[("l2a", tt)])
        M[("l2b", tt)] = I("vector", lambda v: v.scalar_tensor_tensor(
            B2R.ap()[0:1, :], S1R.ap()[0:1, :], -1.0 / D, A2R.ap()[0:1, :],
            op0=ALU.mult, op1=ALU.mult), "dve")

        def emit_l2bc(tt=tt):
            W("tensor", M[("l2b", tt)])
            O("tensor", lambda t: t.matmul(psV.ap()[:, 512:1024], ONESR.ap(),
                                           A2R.ap()[0:1, :], start=True, stop=True))
            M[("l2bc", tt)] = I("tensor", lambda t: t.matmul(
                psT.ap(), ONESR.ap(), B2R.ap()[0:1, :], start=True, stop=True), "pe")

        if tt + 1 < NTT:
            pe_drip.append(emit_l2bc)
        else:
            emit_l2bc()

        def y_b2b(tt=tt):
            W("vector", M[("l2bc", tt)])
            if ("d", tt + 1) in M:
                W("vector", M[("d", tt + 1)])
            guard("vector", lambda v: v.tensor_copy(B2B.ap(), psT.ap()), "dve")

        def y_cc(cc, tt=tt, ts=ts):
            tb = psE.ap() if cc % 2 == 0 else psV.ap()[:, 0:512]
            guard("vector", lambda v, cc=cc, tb=tb: v.tensor_tensor(
                tb, T3[0][:, cc, :], psV.ap()[:, 512:1024], op=ALU.mult), "dve")
            b = (lambda v, cc=cc, tb=tb, ts=ts: v.tensor_tensor(
                X1F3[:, cc, 1 + ts: 513 + ts], tb, B2B.ap(), op=ALU.add))
            if cc == NCH - 1:
                M[("y", tt)] = I("vector", b, "dve")
            else:
                O("vector", b)

        def y_out(tt=tt, ts=ts):
            W("scalar", M[("y", tt)])
            W("scalar", ("dmo", cnt["dmo"]))
            M[("yout", tt)] = I("scalar", lambda sc, ts=ts: sc.dma_start(
                out=y_3[:, :, ts: ts + 512], in_=X1F3[:, :, 1 + ts: 513 + ts]), "dmo")

        if tt + 1 < NTT:
            for _k in range(NCH + 2):
                drip.append(lambda _k=_k, _u=tt + 1: tail_piece_g(_u, _k))
            drip.append(y_b2b)
            for _cc in range(NCH):
                drip.append(lambda _cc=_cc: y_cc(_cc))
            drip.append(y_out)
        else:
            y_b2b()
            for _cc in range(NCH):
                y_cc(_cc)
            y_out()

    ops["sync"].append(("w", "dmo", cnt["dmo"]))

    with nc.Block() as blk:
        def mk(ename):
            def body(eng):
                for item in ops[ename]:
                    if item[0] == "w":
                        eng.wait_ge(sems[item[1]], item[2])
                    elif item[0] == "o":
                        item[1](eng)
                    else:
                        item[1](eng).then_inc(sems[item[2]], item[3])
            return body
        blk.sync(mk("sync"))
        if ops["gpsimd"]:
            blk.gpsimd(mk("gpsimd"))
        blk.tensor(mk("tensor"))
        blk.vector(mk("vector"))
        blk.scalar(mk("scalar"))

    stack.close()
    return nc


_NC_CACHE = []


def kernel(L0_attn_x, L0_attn_W, L0_ffn_W_r, L0_ffn_W_k, L0_ffn_W_v, L0_ffn_mu_r, L0_ffn_mu_k):
    x = np.asarray(L0_attn_x, dtype=np.float32)
    wat = np.ascontiguousarray(np.asarray(L0_attn_W, np.float32).T)
    def _dckmaj(w):
        wt = np.asarray(w, np.float32).T.reshape(8, 128, 8, 128)
        return np.ascontiguousarray(wt.transpose(2, 1, 0, 3))

    wtr = _dckmaj(L0_ffn_W_r)
    wtk = _dckmaj(L0_ffn_W_k)
    wtv = _dckmaj(L0_ffn_W_v)
    mur = np.ascontiguousarray(np.asarray(L0_ffn_mu_r, np.float32).reshape(NCH, 128).T)
    muk = np.ascontiguousarray(np.asarray(L0_ffn_mu_k, np.float32).reshape(NCH, 128).T)
    ident = np.eye(128, dtype=np.float32)
    uu = np.arange(1024)[None, :]
    ii = np.arange(128)[:, None]
    mask = np.where(uu - 384 <= ii, np.float32(0.0), np.float32(-1e30)).astype(ml_dtypes.bfloat16)

    if not _NC_CACHE:
        _NC_CACHE.append(build_program())
    nc = _NC_CACHE[0]
    shared = {"wat": wat, "wtr": wtr, "wtk": wtk, "wtv": wtv, "mur": mur, "muk": muk,
              "identm": ident, "maskadd": mask}
    in_maps = [dict(shared, x0=np.ascontiguousarray(x[b].T)) for b in range(8)]
    res = run_bass_kernel_spmd(nc, in_maps, core_ids=list(range(8)))
    return np.stack([np.asarray(r["y"]).T for r in res.results], axis=0).astype(np.float32)


# version 38
# speedup vs baseline: 1.0238x; 1.0238x over previous
import contextlib
import numpy as np
import ml_dtypes
import concourse.bass as bass
import concourse.mybir as mybir
from concourse.bass_utils import run_bass_kernel_spmd

F32 = mybir.dt.float32
BF16 = mybir.dt.bfloat16
AF = mybir.ActivationFunctionType
ALU = mybir.AluOpType
AX = mybir.AxisListType

S, D = 2048, 1024
NCH = D // 128
NTC = S // 128
NTT = S // 512
CW = 2052
EPS = 1e-5
BASE = 16_416


def _r3(ap, w):
    return ap.rearrange("p (c w) -> p c w", w=w)


def build_program():
    nc = bass.Bass("TRN2", target_bir_lowering=False)

    x0_d = nc.dram_tensor("x0", [D, S], F32, kind="ExternalInput").ap()
    wat_d = nc.dram_tensor("wat", [D, 2 * D], F32, kind="ExternalInput").ap()
    wtr_d = nc.dram_tensor("wtr", [NCH, 128, NCH, 128], F32, kind="ExternalInput").ap()
    wtk_d = nc.dram_tensor("wtk", [NCH, 128, NCH, 128], F32, kind="ExternalInput").ap()
    wtv_d = nc.dram_tensor("wtv", [NCH, 128, NCH, 128], F32, kind="ExternalInput").ap()
    mur_d = nc.dram_tensor("mur", [128, NCH], F32, kind="ExternalInput").ap()
    muk_d = nc.dram_tensor("muk", [128, NCH], F32, kind="ExternalInput").ap()
    ident_d = nc.dram_tensor("identm", [128, 128], F32, kind="ExternalInput").ap()
    mask_d = nc.dram_tensor("maskadd", [128, 1024], BF16, kind="ExternalInput").ap()
    y_d = nc.dram_tensor("y", [D, S], F32, kind="ExternalOutput").ap()
    wvp_d = nc.dram_tensor("wvp", [2 * D, D], BF16, kind="Internal").ap()

    at = nc.alloc_sbuf_tensor_at
    o = BASE
    X0H = at("X0H", [128, NCH * CW], BF16, offset=o)
    X0L = at("X0L", [128, NCH * CW], BF16, offset=o + 32_832)
    KH = at("KH", [128, NCH * 2048], BF16, offset=o + 65_664)
    KL = at("KL", [128, NCH * 2048], BF16, offset=o + 98_432)
    X1F = at("X1F", [128, NCH * 2049], F32, offset=o + 65_664)
    X0F = at("X0F", [128, NCH * 2048], F32, offset=o + 131_200)
    VH = at("VH", [128, NTC * 1024], BF16, offset=o + 131_200)
    VL = at("VL", [128, NTC * 1024], BF16, offset=o + 163_968)
    WQ = at("WQ", [128, NCH * 256], F32, offset=o + 196_736)
    PS = at("PS", [128, 2048], F32, offset=o + 196_736)
    ATOK = at("ATOK", [128, 1024], F32, offset=o + 204_928)
    PT0 = at("PT0", [128, 128], F32, offset=o + 209_024)
    PT1 = at("PT1", [128, 128], F32, offset=o + 209_536)
    MASK = at("MASK", [128, 1024], BF16, offset=o + 210_048)
    IDN = at("IDN", [128, 128], F32, offset=o + 212_096)
    NEGM = at("NEGM", [128, 1], F32, offset=o + 212_608)
    ZCOL = at("ZCOL", [128, 1], F32, offset=o + 212_612, align_bytes=4)
    RZ = at("RZ", [128, 1], F32, offset=o + 212_616, align_bytes=4)
    CONST0 = at("CONST0", [128, 1], F32, offset=o + 212_620, align_bytes=4)
    CONST1 = at("CONST1", [128, 1], F32, offset=o + 212_624, align_bytes=4)
    CONSTB = at("CONSTB", [128, 1], BF16, offset=o + 212_628, align_bytes=4)
    CONSTU = at("CONSTU", [128, 1], mybir.dt.uint8, offset=o + 212_632, align_bytes=4)
    MX4 = at("MX4", [128, 4], F32, offset=o + 212_640, align_bytes=4)
    Z4 = at("Z4", [128, 4], F32, offset=o + 212_656, align_bytes=4)
    LND = at("LND", [128, 1], F32, offset=o + 212_672, align_bytes=4)
    T = [at(f"T{i}", [128, NCH * 512], F32, offset=o + 131_232 + i * 16_384) for i in range(3)]
    T += [at(f"T{i}", [128, NCH * 512], F32, offset=o + (i - 3) * 16_384) for i in range(3, 6)]
    TB = [at(f"TB{i}", [128, NCH * 1024], BF16, offset=o + 131_232 + i * 16_384) for i in range(3)]
    TB += [at(f"TB{i}", [128, NCH * 1024], BF16, offset=o + (i - 3) * 16_384) for i in range(3, 6)]
    WFQE = [at(f"WFQE{h}", [128, 1024], F32, offset=o + 49_152 + h * 4_096) for h in range(2)]
    WFQB = at("WFQB", [128, 4096], BF16, offset=o + 57_344)
    A1R = at("A1R", [1, CW], F32, offset=o + 180_384)
    B1R = at("B1R", [1, CW], F32, offset=o + 188_608)
    SQ8 = at("SQ8", [128, NCH * 256], F32, offset=o + 196_832)
    S1R = at("S1R", [1, 512], F32, offset=o + 196_832)
    S2R = at("S2R", [1, 512], F32, offset=o + 198_880)
    A2R = at("A2R", [1, 512], F32, offset=o + 200_928)
    B2R = at("B2R", [1, 512], F32, offset=o + 202_976)
    MUR = at("MUR", [128, NCH], F32, offset=o + 209_024)
    MUK = at("MUK", [128, NCH], F32, offset=o + 209_056)
    ONESC = at("ONESC", [128, 1], F32, offset=o + 209_088)
    ONESR = at("ONESR", [1, 128], F32, offset=o + 209_120)
    ONESB = at("ONESB", [128, 1], BF16, offset=o + 209_632)

    nc.const_aps.register(F32, 0.0, CONST0.ap())
    nc.const_aps.register(F32, 1.0, CONST1.ap())
    nc.const_aps.register(BF16, 1.0, CONSTB.ap())
    nc.const_aps.register(mybir.dt.uint8, 127, CONSTU.ap())
    LNDV = 0.5 * float(np.log(D))
    nc.const_aps.register(F32, LNDV, LND.ap())

    FSW = at("FSW", [128, 1024], F32, offset=o + 131_200 + 57_344)
    FSH = at("FSH", [128, 1024], BF16, offset=o + 131_200 + 61_440)
    FSL = at("FSL", [128, 1024], BF16, offset=o + 131_200 + 63_488)
    B2B = at("B2B", [128, 512], F32, offset=o + 205_024)
    WQE = at("WQE", [128, 1024], F32, offset=o + 204_928)

    psAB = nc.alloc_psum_tensor("psAB", [128, 2048], F32)
    psE = nc.alloc_psum_tensor("psE", [128, 512], F32)
    psV = nc.alloc_psum_tensor("psV", [128, 1024], F32)
    psT = nc.alloc_psum_tensor("psT", [128, 512], F32)

    XH3 = _r3(X0H.ap(), CW)
    XL3 = _r3(X0L.ap(), CW)
    X0F3 = _r3(X0F.ap(), 2048)
    X1F3 = _r3(X1F.ap(), 2049)
    KH3 = _r3(KH.ap(), 2048)
    KL3 = _r3(KL.ap(), 2048)
    VH3 = _r3(VH.ap(), 1024)
    VL3 = _r3(VL.ap(), 1024)
    PT0B = PT0.ap().bitcast(BF16)
    PT1B = PT1.ap().bitcast(BF16)
    WQ3 = _r3(WQ.ap(), 256)
    WQB4 = WQ.ap().bitcast(BF16).rearrange("p (c s x) -> p c s x", c=NCH, s=2)
    WQE3 = _r3(WQE.ap(), 128)
    wvp4 = wvp_d.rearrange("(c hl p) d -> p c hl d", hl=2, p=128)
    wvp3 = wvp_d.rearrange("(chl p) d -> p chl d", p=128)
    WFQE3 = [_r3(w.ap(), 128) for w in WFQE]
    WFQB4 = WFQB.ap().rearrange("p (c s x) -> p c s x", c=NCH, s=2)
    wfq_ctr = [0]
    T3 = [_r3(t.ap(), 512) for t in T]
    TB3 = [_r3(t.ap(), 1024) for t in TB]
    FSW3 = FSW.ap()
    FSH3 = FSH.ap()
    FSL3 = FSL.ap()

    wat3 = wat_d.rearrange("(c p) d -> p c d", p=128)
    watv3 = wat_d.rearrange("(c p) d -> p c d", p=128)
    wf_src = [wtr_d, wtk_d, wtv_d]
    x0_3 = x0_d.rearrange("(c p) s -> p c s", p=128)
    y_3 = y_d.rearrange("(c p) s -> p c s", p=128)

    stack = contextlib.ExitStack()
    semnames = ("dmx", "dmk", "dmf", "dmo", "dmg", "pe", "dve", "act", "gps")
    sems = {n: stack.enter_context(nc.semaphore(n)) for n in semnames}
    DMAS = ("dmx", "dmk", "dmf", "dmo", "dmg")
    cnt = {k: 0 for k in semnames}
    hw = {}
    ops = {e: [] for e in ("sync", "tensor", "vector", "scalar", "gpsimd")}

    def W(ename, mark):
        if mark is None:
            return
        s, val = mark
        if hw.get((ename, s), 0) >= val:
            return
        hw[(ename, s)] = val
        ops[ename].append(("w", s, val))

    def O(ename, builder):
        ops[ename].append(("o", builder))

    def I(ename, builder, s):
        n = 16 if s in DMAS else 1
        cnt[s] += n
        ops[ename].append(("i", builder, s, n))
        return (s, cnt[s])

    def dmago(stream, out, in_):
        W("sync", (stream, cnt[stream]))
        return I("sync", lambda sy, out=out, in_=in_: sy.dma_start(out=out, in_=in_), stream)

    def guard(ename, builder, s):
        m = I(ename, builder, s)
        W(ename, m)
        return m

    M = {}
    psab_free = {b: None for b in range(4)}
    wf_free = {0: None, 1: None, 2: None}
    wfqb_free = {0: None, 1: None}
    ptb_free = {0: None, 1: None}
    pts_free = {0: None, 1: None}

    dmago("dmx", IDN.ap(), ident_d)
    dmago("dmx", MASK.ap(), mask_d)
    for tt4 in range(4):
        M[("x0s", tt4)] = dmago("dmx" if tt4 % 2 == 0 else "dmk",
                                X0F3[:, :, tt4 * 512:(tt4 + 1) * 512],
                                x0_3[:, :, tt4 * 512:(tt4 + 1) * 512])
    M["const"] = M[("x0s", 2)]
    for tt4 in range(4):
        W("scalar", M[("x0s", tt4)])
        mh = I("scalar", lambda sc, tt4=tt4: sc.copy(
            XH3[:, :, 1 + tt4 * 512: 1 + (tt4 + 1) * 512],
            X0F3[:, :, tt4 * 512:(tt4 + 1) * 512]), "act")
        W("vector", mh)
        M[("x0t", tt4)] = I("vector", lambda v, tt4=tt4: v.scalar_tensor_tensor(
            XL3[:, :, 1 + tt4 * 512: 1 + (tt4 + 1) * 512],
            XH3[:, :, 1 + tt4 * 512: 1 + (tt4 + 1) * 512], -1.0,
            X0F3[:, :, tt4 * 512:(tt4 + 1) * 512], op0=ALU.mult, op1=ALU.add), "dve")
    M["x0"] = M[("x0t", 3)]

    O("vector", lambda v: v.memset(CONST0.ap(), 0.0))
    O("vector", lambda v: v.memset(CONST1.ap(), 1.0))
    O("vector", lambda v: v.memset(CONSTB.ap(), 1.0))
    O("vector", lambda v: v.memset(CONSTU.ap(), 127))
    O("vector", lambda v: v.memset(LND.ap(), LNDV))
    O("vector", lambda v: v.memset(XH3[:, :, 0:1], 0.0))
    M["init"] = I("vector", lambda v: v.memset(XL3[:, :, 0:1], 0.0), "dve")

    def gdma(out, in_):
        W("gpsimd", ("dmg", cnt["dmg"]))
        return I("gpsimd", lambda g, out=out, in_=in_: g.dma_start(out=out, in_=in_), "dmg")

    def split_in(j):
        if j >= NCH:
            return
        if j > 0:
            W("gpsimd", M[("spl_sub", j - 1)])
        else:
            W("gpsimd", M["x0"])
        M[("spl_in", j)] = gdma(FSW3, watv3[:, j, 1024:2048])

    def split_compute(j):
        if j >= NCH:
            return
        W("scalar", M[("spl_in", j)])
        if j > 0:
            W("scalar", M[("spl_out", j - 1)])
        mh = I("scalar", lambda sc: sc.copy(FSH3, FSW3), "act")
        W("gpsimd", mh)
        M[("spl_sub", j)] = I("gpsimd", lambda g: g.tensor_tensor(
            FSL3, FSW3, FSH3, op=ALU.subtract), "gps")
        W("gpsimd", M[("spl_sub", j)])
        gdma(wvp4[:, j, 0, :], FSH3)
        M[("spl_out", j)] = gdma(wvp4[:, j, 1, :], FSL3)
        if j == NCH - 1:
            for qq in range(4):
                M[("wsplitq", 3, qq)] = M[("spl_out", j)]


    M[("kwf", 0)] = dmago("dmk", WQE3[:, :, :], wat3[:, :, 0:128])

    def kw_split(dck):
        s = dck % 2
        W("scalar", M[("kwf", dck)])
        if dck >= 2:
            W("scalar", M[("kmm", dck - 2)])
        mh = I("scalar", lambda sc, s=s: sc.copy(WQB4[:, :, s, 0:128], WQE3[:, :, :]), "act")
        W("vector", mh)
        M[("kw", dck)] = I("vector", lambda v, s=s: v.scalar_tensor_tensor(
            WQB4[:, :, s, 128:256], WQB4[:, :, s, 0:128], -1.0, WQE3[:, :, :],
            op0=ALU.mult, op1=ALU.add), "dve")
        if dck + 1 < NCH:
            W("sync", M[("kw", dck)])
            M[("kwf", dck + 1)] = dmago("dmk", WQE3[:, :, :],
                                        wat3[:, :, (dck + 1) * 128:(dck + 2) * 128])

    kw_split(0)
    for dck in range(NCH):
        s = dck % 2
        W("tensor", M["init"])
        W("tensor", M[("kw", dck)])
        for tt in range(4):
            W("tensor", M[("x0t", tt)])
            W("tensor", psab_free[tt])
            n = 0
            for cc in range(NCH):
                for (wr, xw) in ((0, XH3), (0, XL3), (1, XH3)):
                    n += 1
                    b = (lambda t, tt=tt, cc=cc, s=s, wr=wr, xw=xw,
                         first=(n == 1), lastmm=(n == 24):
                         t.matmul(psAB.ap()[:, tt * 512:(tt + 1) * 512],
                                  WQB4[:, cc, s, wr * 128:(wr + 1) * 128],
                                  xw[:, cc, 1 + tt * 512: 1 + (tt + 1) * 512],
                                  start=first, stop=lastmm))
                    if n == 24:
                        M[("kps", dck, tt)] = I("tensor", b, "pe")
                    else:
                        O("tensor", b)
        M[("kmm", dck)] = M[("kps", dck, 3)]
        if dck + 1 < NCH:
            kw_split(dck + 1)
        for tt in range(4):
            W("vector", M[("kps", dck, tt)])
            guard("vector", lambda v, dck=dck, tt=tt: v.tensor_copy(
                KH3[:, dck, tt * 512:(tt + 1) * 512],
                psAB.ap()[:, tt * 512:(tt + 1) * 512]), "dve")
            psab_free[tt] = I("vector", lambda v, dck=dck, tt=tt: v.scalar_tensor_tensor(
                KL3[:, dck, tt * 512:(tt + 1) * 512],
                KH3[:, dck, tt * 512:(tt + 1) * 512], -1.0,
                psAB.ap()[:, tt * 512:(tt + 1) * 512],
                op0=ALU.mult, op1=ALU.add), "dve")
        split_in(dck)
        split_compute(dck)

    WQ16 = WQ.ap().bitcast(BF16).rearrange("p (chl x) -> p chl x", x=256)
    for q in range(4):
        W("sync", M[("kmm", NCH - 1)] if q == 0 else M[("vmm", q - 1)])
        W("sync", M[("wsplitq", 3, q)])
        M[("vw", q)] = dmago("dmk", WQ16[:, 0:8, :],
                             wvp3[:, 0:8, q * 256:(q + 1) * 256])
        M[("vwb", q)] = dmago("dmx", WQ16[:, 8:16, :],
                              wvp3[:, 8:16, q * 256:(q + 1) * 256])
        for tc in range(NTC):
            bank = tc % 4
            W("tensor", M[("vw", q)])
            W("tensor", M[("vwb", q)])
            W("tensor", M[("x0t", tc // 4)])
            W("tensor", psab_free[bank])
            n = 0
            for cc in range(NCH):
                for (xw, wr) in ((XH3, 0), (XH3, 1), (XL3, 0)):
                    n += 1
                    b = (lambda t, bank=bank, cc=cc, tc=tc, xw=xw, wr=wr,
                         first=(n == 1), lastmm=(n == 24):
                         t.matmul(psAB.ap()[:, bank * 512: bank * 512 + 256],
                                  xw[:, cc, 1 + tc * 128: 1 + (tc + 1) * 128],
                                  WQ16[:, cc * 2 + wr, :],
                                  start=first, stop=lastmm))
                    if n == 24:
                        M[("vps", q, tc)] = I("tensor", b, "pe")
                    else:
                        O("tensor", b)
            W("vector", M[("vps", q, tc)])
            if tc >= 12:
                W("vector", M[("spl_out", NCH - 1)])
            guard("vector", lambda v, tc=tc, q=q, bank=bank: v.tensor_copy(
                VH3[:, tc, q * 256:(q + 1) * 256],
                psAB.ap()[:, bank * 512: bank * 512 + 256]), "dve")
            psab_free[bank] = I("vector",
                                lambda v, tc=tc, q=q, bank=bank: v.scalar_tensor_tensor(
                                    VL3[:, tc, q * 256:(q + 1) * 256],
                                    VH3[:, tc, q * 256:(q + 1) * 256], -1.0,
                                    psAB.ap()[:, bank * 512: bank * 512 + 256],
                                    op0=ALU.mult, op1=ALU.add), "dve")
        M[("vmm", q)] = M[("vps", q, NTC - 1)]

    pending_at = None
    for qc in range(NTC):
        kt_n = qc // 4 + 1
        r = qc % 4
        kw = kt_n * 512
        nkc = qc + 1

        korder = [kt_n - 1] + list(range(kt_n - 1))
        W("tensor", M["const"])
        for ki, kt in enumerate(korder):
            W("tensor", psab_free[kt])
            n = 0
            for dc in range(NCH):
                for (xw, kw3) in ((XH3, KH3), (XH3, KL3), (XL3, KH3)):
                    n += 1
                    b = (lambda t, dc=dc, kt=kt, qc=qc, xw=xw, kw3=kw3,
                         first=(n == 1), lastmm=(n == 24):
                         t.matmul(psAB.ap()[:, kt * 512:(kt + 1) * 512],
                                  xw[:, dc, 1 + qc * 128: 1 + (qc + 1) * 128],
                                  kw3[:, dc, kt * 512:(kt + 1) * 512],
                                  start=first, stop=lastmm))
                    if n == 24:
                        M[("Sb", qc, kt)] = I("tensor", b, "pe")
                    else:
                        O("tensor", b)
            if ki == 0 and pending_at is not None:
                pending_at()
                pending_at = None

        W("vector", M["const"])
        for kt in korder:
            W("vector", M[("Sb", qc, kt)])
            if kt == kt_n - 1:
                guard("vector", (lambda v, kt_n=kt_n, r=r: v.tensor_tensor(
                    psAB.ap()[:, (kt_n - 1) * 512: kt_n * 512],
                    psAB.ap()[:, (kt_n - 1) * 512: kt_n * 512],
                    MASK.ap()[:, 384 - 128 * r: 896 - 128 * r], op=ALU.add)), "dve")
            b = (lambda v, kt=kt: v.reduce_max(
                MX4.ap()[:, kt:kt + 1], psAB.ap()[:, kt * 512:(kt + 1) * 512],
                axis=AX.X))
            guard("vector", b, "dve")
        M[("negm", qc)] = I("vector", lambda v, kt_n=kt_n: v.reduce_max(
            NEGM.ap(), MX4.ap()[:, 0:kt_n], axis=AX.X, negate=True), "dve")

        W("scalar", M["init"])
        W("scalar", M[("negm", qc)])
        for kt in range(kt_n):
            if qc > 0:
                kcp = min(4 * kt + 3, qc - 1)
                W("scalar", M[("ptt", qc - 1, kcp)])
            M[("expb", qc, kt)] = I("scalar", lambda sc, kt=kt: sc.activation(
                out=PS.ap()[:, kt * 512:(kt + 1) * 512],
                in_=psAB.ap()[:, kt * 512:(kt + 1) * 512],
                func=AF.Exp, bias=NEGM.ap(), scale=1.0,
                accum_out=Z4.ap()[:, kt:kt + 1]), "act")
            psab_free[kt] = M[("expb", qc, kt)]
        M[("exp", qc)] = M[("expb", qc, kt_n - 1)]

        W("vector", M[("exp", qc)])
        W("vector", M.get(("aev", qc - 1)))
        guard("vector", lambda v, kt_n=kt_n: v.reduce_sum(
            ZCOL.ap(), Z4.ap()[:, 0:kt_n], axis=AX.X), "dve")
        M[("rz", qc)] = I("vector", lambda v: v.reciprocal(RZ.ap(), ZCOL.ap()), "dve")

        def emit_T(kc):
            par = kc % 2
            ptb = psE if par == 0 else psT
            W("tensor", M[("expb", qc, kc // 4)])
            W("tensor", ptb_free[par])
            M[("ptt", qc, kc)] = I("tensor", lambda t, kc=kc, ptb=ptb: t.transpose(
                ptb.ap()[:, 0:128], PS.ap()[:, kc * 128:(kc + 1) * 128], IDN.ap()), "pe")

        emit_T(0)
        for kc in range(nkc):
            par = kc % 2
            ptb = psE if par == 0 else psT
            pts = PT0 if par == 0 else PT1

            ptsb = PT0B if par == 0 else PT1B
            W("vector", M[("ptt", qc, kc)])
            W("vector", pts_free[par])
            mcp = guard("vector", lambda v, ptb=ptb, ptsb=ptsb:
                        v.tensor_copy(ptsb[:, 0:128], ptb.ap()[:, 0:128]), "dve")
            ptb_free[par] = M[("ptev", qc, kc)] = I(
                "vector", lambda v, ptb=ptb, ptsb=ptsb: v.scalar_tensor_tensor(
                    ptsb[:, 128:256], ptsb[:, 0:128], -1.0, ptb.ap()[:, 0:128],
                    op0=ALU.mult, op1=ALU.add), "dve")

            if kc + 1 < nkc:
                emit_T(kc + 1)

            W("tensor", mcp)
            if kc == 0:
                W("tensor", M.get(("aev", qc - 1)))
            first, last = kc == 0, kc == nkc - 1
            for n6, (pofs, vw, dh) in enumerate((
                    (0, VH3, 0), (0, VH3, 1), (0, VL3, 0), (0, VL3, 1),
                    (128, VH3, 0), (128, VH3, 1))):
                if n6 == 4:
                    W("tensor", M[("ptev", qc, kc)])
                b = (lambda t, kc=kc, ptsb=ptsb, pofs=pofs, vw=vw, dh=dh,
                     first=first, last=last, n6=n6: t.matmul(
                         psV.ap()[:, dh * 512:(dh + 1) * 512],
                         ptsb[:, pofs:pofs + 128], vw[:, kc, dh * 512:(dh + 1) * 512],
                         start=(first and n6 < 2), stop=(last and n6 >= 4)))
                if n6 == 5:
                    mark = I("tensor", b, "pe")
                elif last and n6 == 4:
                    M[("pvh", qc)] = I("tensor", b, "pe")
                else:
                    O("tensor", b)
            pts_free[par] = mark
            if kc == nkc - 1:
                M[("pv", qc)] = mark

        W("vector", M[("pvh", qc)])
        W("vector", M[("rz", qc)])
        M[("aevh", qc)] = I("vector", lambda v: v.tensor_scalar_mul(
            ATOK.ap()[:, 0:512], psV.ap()[:, 0:512], RZ.ap()), "dve")
        W("vector", M[("pv", qc)])
        M[("aev", qc)] = I("vector", lambda v: v.tensor_scalar_mul(
            ATOK.ap()[:, 512:1024], psV.ap()[:, 512:1024], RZ.ap()), "dve")

        def emit_at_cmb(qc=qc):
            W("tensor", M[("aevh", qc)])
            W("tensor", ptb_free[0])
            W("tensor", ptb_free[1])
            for dc in range(NCH):
                if dc == 4:
                    W("tensor", M[("aev", qc)])
                tgt = psE if dc < 4 else psT
                b = (lambda t, dc=dc, tgt=tgt: t.transpose(
                    tgt.ap()[:, (dc % 4) * 128: (dc % 4 + 1) * 128],
                    ATOK.ap()[:, dc * 128:(dc + 1) * 128], IDN.ap()))
                if dc == NCH - 1:
                    M[("at", qc)] = I("tensor", b, "pe")
                else:
                    O("tensor", b)

            W("vector", M[("at", qc)])
            W("vector", M["x0"])
            atok3 = _r3(ATOK.ap(), 128)
            pse3 = _r3(psE.ap(), 128)
            pst3 = _r3(psT.ap(), 128)
            guard("vector", lambda v, qc=qc: v.tensor_tensor(
                atok3, XH3[:, :, 1 + qc * 128: 1 + (qc + 1) * 128],
                XL3[:, :, 1 + qc * 128: 1 + (qc + 1) * 128], op=ALU.add), "dve")
            guard("vector", lambda v: v.scalar_tensor_tensor(
                _r3(ATOK.ap()[:, 0:512], 128), _r3(ATOK.ap()[:, 0:512], 128), 2.0,
                pse3, op0=ALU.mult, op1=ALU.add), "dve")
            mA = guard("vector", lambda v: v.scalar_tensor_tensor(
                _r3(ATOK.ap()[:, 512:1024], 128), _r3(ATOK.ap()[:, 512:1024], 128),
                2.0, pst3, op0=ALU.mult, op1=ALU.add), "dve")
            ptb_free[0] = ptb_free[1] = mA
            guard("vector", lambda v, qc=qc: v.tensor_copy(
                XH3[:, :, 1 + qc * 128: 1 + (qc + 1) * 128], atok3), "dve")
            M[("cmb", qc)] = I("vector", lambda v, qc=qc: v.scalar_tensor_tensor(
                XL3[:, :, 1 + qc * 128: 1 + (qc + 1) * 128],
                XH3[:, :, 1 + qc * 128: 1 + (qc + 1) * 128], -1.0, atok3,
                op0=ALU.mult, op1=ALU.add), "dve")

        if qc + 1 < NTC and (qc + 1) // 4 + 1 >= 3:
            pending_at = emit_at_cmb
        else:
            emit_at_cmb()
    psab_free[2] = psab_free[3] = M[("cmb", NTC - 1)]
    psab_free[0] = psab_free[1] = M[("cmb", NTC - 1)]

    W("sync", M[("at", NTC - 1)])
    dmago("dmx", MUR.ap(), mur_d)
    M["mu"] = dmago("dmx", MUK.ap(), muk_d)

    W("vector", M[("cmb", NTC - 1)])
    O("vector", lambda v: v.memset(ONESR.ap(), 1.0))
    O("vector", lambda v: v.memset(ONESB.ap(), 1.0))
    M["pc0"] = I("vector", lambda v: v.memset(ONESC.ap(), 1.0), "dve")

    SQ83 = _r3(SQ8.ap(), 256)
    O("vector", lambda v: v.memset(X1F3[:, :, 0:1], 0.0))
    for tt in range(NTT):
        W("vector", M[("cmb", NTC - 1)])
        W("vector", M["pc0"])
        M[("rec", tt)] = guard("vector", lambda v, tt=tt: v.tensor_tensor(
            X1F3[:, :, 1 + tt * 512: 1 + (tt + 1) * 512],
            XH3[:, :, 1 + tt * 512: 1 + (tt + 1) * 512],
            XL3[:, :, 1 + tt * 512: 1 + (tt + 1) * 512], op=ALU.add), "dve")
        W("tensor", M[("rec", tt)])
        W("tensor", M["pc0"])
        W("tensor", psab_free[tt])
        for cc in range(NCH):
            b = (lambda t, cc=cc, tt=tt, first=(cc == 0), lastmm=(cc == NCH - 1):
                 t.matmul(psAB.ap()[0:1, tt * 512:(tt + 1) * 512], ONESC.ap(),
                          X1F3[:, cc, 1 + tt * 512: 1 + (tt + 1) * 512],
                          start=first, stop=lastmm))
            if cc == NCH - 1:
                M[("s1", tt)] = I("tensor", b, "pe")
            else:
                O("tensor", b)
        s2bank = (psE.ap() if tt == 0 else
                  psV.ap()[:, 0:512] if tt == 1 else
                  psV.ap()[:, 512:1024] if tt == 2 else psT.ap())
        for hf in range(2):
            ix = tt * 2 + hf
            s2ap = s2bank[0:1, hf * 256:(hf + 1) * 256]
            sqb = SQ83 if ix % 2 == 0 else _r3(T[1].ap()[:, 0:2048], 256)
            W("scalar", M[("rec", tt)])
            if ix >= 2:
                W("scalar", M[("s2h", (ix - 2) // 2, (ix - 2) % 2)])
            msq = I("scalar", lambda sc, tt=tt, hf=hf, sqb=sqb: sc.activation(
                out=sqb[:, :, :],
                in_=X1F3[:, :, 1 + tt * 512 + hf * 256: 1 + tt * 512 + (hf + 1) * 256],
                func=AF.Square), "act")
            W("tensor", msq)
            for cc in range(NCH):
                last = cc == NCH - 1
                b = (lambda t, cc=cc, s2ap=s2ap, sqb=sqb: t.matmul(
                    s2ap, ONESC.ap(), sqb[:, cc, :],
                    start=(cc == 0), stop=(cc == NCH - 1)))
                if last:
                    M[("s2h", tt, hf)] = I("tensor", b, "pe")
                else:
                    O("tensor", b)
        W("vector", M[("s2h", tt, 1)])
        M[("s2ev", tt)] = I("vector", lambda v, tt=tt, s2bank=s2bank: v.tensor_copy(
            B1R.ap()[0:1, 1 + tt * 512: 1 + (tt + 1) * 512],
            s2bank[0:1, 0:512]), "dve")

        W("vector", M[("s1", tt)])
        psab_free[tt] = M[("s1ev", tt)] = I("vector", lambda v, tt=tt: v.tensor_copy(
            A1R.ap()[0:1, 1 + tt * 512: 1 + (tt + 1) * 512],
            psAB.ap()[0:1, tt * 512:(tt + 1) * 512]), "dve")

    r1 = SQ8.ap()[0:1, 0:2048]
    a1 = A1R.ap()[0:1, 1:2049]
    b1 = B1R.ap()[0:1, 1:2049]

    for _tt in range(NTT):
        W("vector", M[("s1ev", _tt)])
        W("vector", M[("s2ev", _tt)])
    O("vector", lambda v: v.tensor_scalar_mul(a1, a1, 1.0 / D))
    guard("vector", lambda v: v.tensor_scalar_mul(b1, b1, 1.0 / D), "dve")
    guard("vector", lambda v: v.tensor_tensor(r1, a1, a1, op=ALU.mult), "dve")
    guard("vector", lambda v: v.tensor_tensor(b1, b1, r1, op=ALU.subtract), "dve")
    M["ln1v"] = I("vector", lambda v: v.tensor_scalar_add(b1, b1, EPS), "dve")

    W("scalar", M["ln1v"])
    guard("scalar", lambda sc: sc.activation(out=r1, in_=b1, func=AF.Ln), "act")
    M["ln1a"] = I("scalar", lambda sc: sc.activation(out=r1, in_=r1, func=AF.Exp, scale=-0.5), "act")

    W("vector", M["ln1a"])
    guard("vector", lambda v: v.scalar_tensor_tensor(b1, a1, -1.0, r1, op0=ALU.mult, op1=ALU.mult), "dve")
    O("vector", lambda v: v.tensor_copy(a1, r1))
    O("vector", lambda v: v.memset(A1R.ap()[0:1, 0:1], 0.0))
    O("vector", lambda v: v.memset(A1R.ap()[0:1, 2049:CW], 0.0))
    O("vector", lambda v: v.memset(B1R.ap()[0:1, 0:1], 0.0))
    M["ln1"] = I("vector", lambda v: v.memset(B1R.ap()[0:1, 2049:CW], 0.0), "dve")

    def x1n_bc(t4):
        ts4 = t4 * 512
        pea = psE.ap() if t4 % 2 == 0 else psV.ap()[:, 512:1024]
        peb = psV.ap()[:, 0:512] if t4 % 2 == 0 else psT.ap()
        W("tensor", M["ln1"])
        if t4 >= 2:
            W("tensor", M[("x1n", t4 - 2)])
        O("tensor", lambda t, ts4=ts4, pea=pea: t.matmul(
            pea, ONESR.ap(), A1R.ap()[0:1, 1 + ts4: 513 + ts4], start=True, stop=True))
        M[("x1bc", t4)] = I("tensor", lambda t, ts4=ts4, peb=peb: t.matmul(
            peb, ONESR.ap(), B1R.ap()[0:1, 1 + ts4: 513 + ts4], start=True, stop=True), "pe")

    def x1n_cc(t4, cc):
        ts4 = t4 * 512
        pea = psE.ap() if t4 % 2 == 0 else psV.ap()[:, 512:1024]
        peb = psV.ap()[:, 0:512] if t4 % 2 == 0 else psT.ap()
        if cc == 0:
            W("vector", M[("x1bc", t4)])
        guard("vector", lambda v, cc=cc, ts4=ts4, pea=pea: v.tensor_tensor(
            X1F3[:, cc, 1 + ts4: 513 + ts4], X1F3[:, cc, 1 + ts4: 513 + ts4],
            pea, op=ALU.mult), "dve")
        b = (lambda v, cc=cc, ts4=ts4, peb=peb: v.tensor_tensor(
            X1F3[:, cc, 1 + ts4: 513 + ts4], X1F3[:, cc, 1 + ts4: 513 + ts4],
            peb, op=ALU.add))
        if cc == NCH - 1:
            M[("x1n", t4)] = guard("vector", b, "dve")
        else:
            O("vector", b)

    x1n_bc(0)
    for _cc in range(NCH):
        x1n_cc(0, _cc)

    def emit_prep_head(u):
        ts = u * 512
        W("vector", M[("x1n", u)])
        if u > 0:
            W("vector", M[("x1n", u - 1)])
        W("vector", M["mu"])
        if u > 0:
            W("vector", M[("mm", "v", NCH - 1, u - 1)])
            W("vector", M[("x2p", u - 1)])
        M[("d", u)] = guard("vector", lambda v, ts=ts: v.tensor_tensor(
            T3[2][:, :, :], X1F3[:, :, 1 + ts: 513 + ts], X1F3[:, :, ts: 512 + ts],
            op=ALU.subtract), "dve")
        for cc in range(NCH):
            b = (lambda v, cc=cc, ts=ts: v.scalar_tensor_tensor(
                T3[5][:, cc, :], T3[2][:, cc, :], MUR.ap()[:, cc:cc + 1],
                X1F3[:, cc, ts: 512 + ts], op0=ALU.mult, op1=ALU.add))
            if cc == NCH - 1:
                guard("vector", b, "dve")
            else:
                O("vector", b)
        guard("vector", lambda v: v.tensor_copy(
            TB3[3][:, :, 0:512], T3[5][:, :, :]), "dve")
        M[("xr", u)] = guard("vector", lambda v: v.scalar_tensor_tensor(
            TB3[3][:, :, 512:1024], TB3[3][:, :, 0:512],
            -1.0, T3[5][:, :, :], op0=ALU.mult, op1=ALU.add), "dve")

    def tail_piece_g(u, k):
        ts2 = u * 512
        if k == 0:
            W("vector", M[("xr", u)])
        if k < NCH:
            b = (lambda v, cc=k, ts2=ts2: v.scalar_tensor_tensor(
                T3[5][:, cc, :], T3[2][:, cc, :], MUK.ap()[:, cc:cc + 1],
                X1F3[:, cc, ts2: 512 + ts2], op0=ALU.mult, op1=ALU.add))
            if k == NCH - 1:
                guard("vector", b, "dve")
            else:
                O("vector", b)
        elif k == NCH:
            guard("vector", lambda v: v.tensor_copy(
                TB3[4][:, :, 0:512], T3[5][:, :, :]), "dve")
        else:
            M[("xk", u)] = guard("vector", lambda v: v.scalar_tensor_tensor(
                TB3[4][:, :, 512:1024], TB3[4][:, :, 0:512],
                -1.0, T3[5][:, :, :], op0=ALU.mult, op1=ALU.add), "dve")

    emit_prep_head(0)
    drip = []
    pe_drip = []
    for _k in range(NCH + 2):
        drip.append(lambda _k=_k: tail_piece_g(0, _k))
    for _t4 in range(1, NTT):
        drip.append(lambda _t4=_t4: x1n_bc(_t4))
        for _cc in range(NCH):
            drip.append(lambda _t4=_t4, _cc=_cc: x1n_cc(_t4, _cc))

    for tt in range(NTT):
        ts = tt * 512

        def wfq_load(name, wi, dck, tt):
            h = wfq_ctr[0] % 2
            wfq_ctr[0] += 1
            W("sync", wf_free[h])
            W("sync", M[("rec", NTT - 1)])
            M[("wfq", name, dck, tt)] = dmago(
                "dmf" if h == 0 else "dmx", WFQE3[h][:, :, :],
                wf_src[wi][dck, :, :, :])
            M[("wfqh", name, dck, tt)] = h

        def wfq_split(name, dck, tt):
            s = dck % 2
            h = M[("wfqh", name, dck, tt)]
            W("scalar", M[("wfq", name, dck, tt)])
            W("scalar", wfqb_free[s])
            mh = I("scalar", lambda sc, s=s, h=h: sc.copy(
                WFQB4[:, :, s, 0:128], WFQE3[h][:, :, :]), "act")
            W("vector", mh)
            wf_free[h] = M[("wfl", name, dck, tt)] = I(
                "vector", lambda v, s=s, h=h: v.scalar_tensor_tensor(
                    WFQB4[:, :, s, 128:256], WFQB4[:, :, s, 0:128], -1.0,
                    WFQE3[h][:, :, :], op0=ALU.mult, op1=ALU.add), "dve")

        def emit_chain(name, wi, rhs_tb, rhs_mark, evac_fn, nxt, tt=tt):
            for d0 in (0, 1):
                if ("wfq", name, d0, tt) not in M:
                    wfq_load(name, wi, d0, tt)
            if ("wfl", name, 0, tt) not in M:
                wfq_split(name, 0, tt)
            for dck in range(NCH):
                bank = dck % 4
                s = dck % 2
                W("tensor", M[("wfl", name, dck, tt)])
                W("tensor", rhs_mark)
                W("tensor", psab_free[bank])
                n = 0
                for cc in range(NCH):
                    for (wrow, xofs) in ((0, 0), (0, 512), (1, 0)):
                        n += 1
                        lastmm = n == 24
                        b = (lambda t, cc=cc, s=s, bank=bank, wrow=wrow,
                             xofs=xofs, rhs_tb=rhs_tb, first=(n == 1), lastmm=lastmm:
                             t.matmul(psAB.ap()[:, bank * 512:(bank + 1) * 512],
                                      WFQB4[:, cc, s, wrow * 128:(wrow + 1) * 128],
                                      TB3[rhs_tb][:, cc, xofs: xofs + 512],
                                      start=first, stop=lastmm))
                        if lastmm:
                            M[("mm", name, dck, tt)] = I("tensor", b, "pe")
                        else:
                            O("tensor", b)
                wfqb_free[s] = M[("mm", name, dck, tt)]
                if dck + 1 < NCH and ("wfl", name, dck + 1, tt) not in M:
                    wfq_split(name, dck + 1, tt)
                if dck + 2 < NCH:
                    wfq_load(name, wi, dck + 2, tt)
                if nxt is not None and dck in (5, 6):
                    wfq_load(nxt[0], nxt[1], dck - 5, nxt[2])
                if nxt is not None and dck in (6, 7):
                    wfq_split(nxt[0], dck - 6, nxt[2])
                evac_fn(dck)
                if pe_drip and dck == 2:
                    pe_drip.pop(0)()
                for _ in range(3 if name == "r" else 2 if name == "k" else 0):
                    if drip:
                        drip.pop(0)()

        def ev_r(dc, tt=tt):
            W("scalar", M[("mm", "r", dc, tt)])
            psab_free[dc % 4] = M[("ev", "r", dc, tt)] = I("scalar", lambda sc, dc=dc: sc.activation(
                out=T3[1][:, dc, :], in_=psAB.ap()[:, (dc % 4) * 512:(dc % 4 + 1) * 512],
                func=AF.Sigmoid), "act")

        emit_chain("r", 0, 3, M[("xr", tt)], ev_r, ("k", 1, tt))

        def ev_k(dc, tt=tt):
            W("scalar", M[("mm", "k", dc, tt)])
            psab_free[dc % 4] = M[("ev", "k", dc, tt)] = I("scalar", lambda sc, dc=dc: sc.activation(
                out=T3[2][:, dc, :], in_=psAB.ap()[:, (dc % 4) * 512:(dc % 4 + 1) * 512],
                func=AF.Relu), "act")
            if dc in (3, 7):
                hf = dc // 4
                c0, c1 = hf * 4, hf * 4 + 4
                W("scalar", M[("ev", "k", dc, tt)])
                mkf = I("scalar", lambda sc, c0=c0, c1=c1: sc.activation(
                    out=T3[5][:, c0:c1, :], in_=T3[2][:, c0:c1, :], func=AF.Square), "act")
                W("vector", mkf)
                if hf == 0:
                    W("vector", M[("mm", "r", NCH - 1, tt)])
                guard("vector", lambda v, c0=c0, c1=c1: v.tensor_copy(
                    TB3[3][:, c0:c1, 0:512], T3[5][:, c0:c1, :]), "dve")
                m = guard("vector", lambda v, c0=c0, c1=c1: v.scalar_tensor_tensor(
                    TB3[3][:, c0:c1, 512:1024], TB3[3][:, c0:c1, 0:512], -1.0,
                    T3[5][:, c0:c1, :], op0=ALU.mult, op1=ALU.add), "dve")
                if hf == 1:
                    M[("ka", tt)] = m

        emit_chain("k", 1, 4, M[("xk", tt)], ev_k, ("v", 2, tt))

        def ev_v(dc, tt=tt):
            W("vector", M[("mm", "v", dc, tt)])
            guard("vector", lambda v, dc=dc: v.tensor_tensor(
                T3[2][:, dc, :], psAB.ap()[:, (dc % 4) * 512:(dc % 4 + 1) * 512],
                T3[1][:, dc, :], op=ALU.mult), "dve")
            psab_free[dc % 4] = M[("ev", "v", dc, tt)] = I("vector", lambda v, dc=dc, ts=ts: v.tensor_tensor(
                T3[0][:, dc, :], X1F3[:, dc, 1 + ts: 513 + ts], T3[2][:, dc, :], op=ALU.add), "dve")
            if dc in (3, 7) and tt + 1 < NTT:
                u = tt + 1
                ts2 = u * 512
                hf = dc // 4
                c0, c1 = hf * 4, hf * 4 + 4
                W("vector", M[("x1n", u)])
                W("vector", M[("x1n", tt)])
                W("vector", M["mu"])
                W("vector", M[("ev", "v", dc, tt)])
                m_d = guard("vector", lambda v, c0=c0, c1=c1, ts2=ts2: v.tensor_tensor(
                    T3[2][:, c0:c1, :], X1F3[:, c0:c1, 1 + ts2: 513 + ts2],
                    X1F3[:, c0:c1, ts2: 512 + ts2], op=ALU.subtract), "dve")
                if hf == 1:
                    M[("d", u)] = m_d
                for cc in range(c0, c1):
                    b = (lambda v, cc=cc, ts2=ts2: v.scalar_tensor_tensor(
                        T3[5][:, cc, :], T3[2][:, cc, :], MUR.ap()[:, cc:cc + 1],
                        X1F3[:, cc, ts2: 512 + ts2], op0=ALU.mult, op1=ALU.add))
                    if cc == c1 - 1:
                        guard("vector", b, "dve")
                    else:
                        O("vector", b)
                if hf == 1:
                    W("vector", M[("mm", "v", NCH - 1, tt)])
                    guard("vector", lambda v: v.tensor_copy(
                        TB3[3][:, :, 0:512], T3[5][:, :, :]), "dve")
                    M[("xr", u)] = guard("vector", lambda v: v.scalar_tensor_tensor(
                        TB3[3][:, :, 512:1024], TB3[3][:, :, 0:512],
                        -1.0, T3[5][:, :, :], op0=ALU.mult, op1=ALU.add), "dve")

        emit_chain("v", 2, 3, M[("ka", tt)], ev_v,
                   ("r", 0, tt + 1) if tt + 1 < NTT else None)
        M[("x2p", tt)] = M[("ev", "v", NCH - 1, tt)]

        for hf in range(2):
            c0, c1 = hf * 4, hf * 4 + 4
            W("scalar", M[("ev", "v", c1 - 1, tt)])
            M[("sq2", tt, hf)] = I("scalar", lambda sc, c0=c0, c1=c1: sc.activation(
                out=T3[1][:, c0:c1, :], in_=T3[0][:, c0:c1, :], func=AF.Square), "act")

        if tt > 0:
            W("tensor", M[("y", tt - 1)])
        W("tensor", M[("x1n", NTT - 1)])
        for cc in range(NCH):
            last = cc == NCH - 1
            W("tensor", M[("ev", "v", cc, tt)])
            b = (lambda t, cc=cc: t.matmul(psE.ap()[0:1, 0:512], ONESC.ap(), T3[0][:, cc, :],
                                           start=(cc == 0), stop=(cc == NCH - 1)))
            if last:
                M[("l2s1", tt)] = I("tensor", b, "pe")
            else:
                O("tensor", b)
        for cc in range(NCH):
            last = cc == NCH - 1
            W("tensor", M[("sq2", tt, cc // 4)])
            b = (lambda t, cc=cc: t.matmul(psV.ap()[0:1, 0:512], ONESC.ap(), T3[1][:, cc, :],
                                           start=(cc == 0), stop=(cc == NCH - 1)))
            if last:
                M[("l2s2", tt)] = I("tensor", b, "pe")
            else:
                O("tensor", b)

        W("vector", M[("l2s2", tt)])
        s1 = S1R.ap()[0:1, :]
        s2 = S2R.ap()[0:1, :]
        guard("vector", lambda v: v.tensor_copy(s1, psE.ap()[0:1, 0:512]), "dve")
        guard("vector", lambda v: v.tensor_tensor(A2R.ap()[0:1, :], s1, s1, op=ALU.mult), "dve")
        guard("vector", lambda v: v.scalar_tensor_tensor(
            s2, A2R.ap()[0:1, :], -1.0 / D, psV.ap()[0:1, 0:512],
            op0=ALU.mult, op1=ALU.add), "dve")
        M[("l2v", tt)] = I("vector", lambda v: v.tensor_scalar_add(s2, s2, D * EPS), "dve")

        W("scalar", M[("l2v", tt)])
        guard("scalar", lambda sc: sc.activation(out=A2R.ap()[0:1, :], in_=S2R.ap()[0:1, :],
                                                 func=AF.Ln), "act")
        M[("l2a", tt)] = I("scalar", lambda sc: sc.activation(
            out=A2R.ap()[0:1, :], in_=A2R.ap()[0:1, :], func=AF.Exp, scale=-0.5,
            bias=LNDV), "act")

        W("vector", M[("l2a", tt)])
        M[("l2b", tt)] = I("vector", lambda v: v.scalar_tensor_tensor(
            B2R.ap()[0:1, :], S1R.ap()[0:1, :], -1.0 / D, A2R.ap()[0:1, :],
            op0=ALU.mult, op1=ALU.mult), "dve")

        def emit_l2bc(tt=tt):
            W("tensor", M[("l2b", tt)])
            O("tensor", lambda t: t.matmul(psV.ap()[:, 512:1024], ONESR.ap(),
                                           A2R.ap()[0:1, :], start=True, stop=True))
            M[("l2bc", tt)] = I("tensor", lambda t: t.matmul(
                psT.ap(), ONESR.ap(), B2R.ap()[0:1, :], start=True, stop=True), "pe")

        if tt + 1 < NTT:
            pe_drip.append(emit_l2bc)
        else:
            emit_l2bc()

        def y_b2b(tt=tt):
            W("vector", M[("l2bc", tt)])
            if ("d", tt + 1) in M:
                W("vector", M[("d", tt + 1)])
            guard("vector", lambda v: v.tensor_copy(B2B.ap(), psT.ap()), "dve")

        def y_cc(cc, tt=tt, ts=ts):
            tb = psE.ap() if cc % 2 == 0 else psV.ap()[:, 0:512]
            guard("vector", lambda v, cc=cc, tb=tb: v.tensor_tensor(
                tb, T3[0][:, cc, :], psV.ap()[:, 512:1024], op=ALU.mult), "dve")
            b = (lambda v, cc=cc, tb=tb, ts=ts: v.tensor_tensor(
                X1F3[:, cc, 1 + ts: 513 + ts], tb, B2B.ap(), op=ALU.add))
            if cc == NCH - 1:
                M[("y", tt)] = I("vector", b, "dve")
            else:
                O("vector", b)

        def y_out(tt=tt, ts=ts):
            W("scalar", M[("y", tt)])
            W("scalar", ("dmo", cnt["dmo"]))
            M[("yout", tt)] = I("scalar", lambda sc, ts=ts: sc.dma_start(
                out=y_3[:, :, ts: ts + 512], in_=X1F3[:, :, 1 + ts: 513 + ts]), "dmo")

        if tt + 1 < NTT:
            for _k in range(NCH + 2):
                drip.append(lambda _k=_k, _u=tt + 1: tail_piece_g(_u, _k))
            drip.append(y_b2b)
            for _cc in range(NCH):
                drip.append(lambda _cc=_cc: y_cc(_cc))
            drip.append(y_out)
        else:
            y_b2b()
            for _cc in range(NCH):
                y_cc(_cc)
            y_out()

    ops["sync"].append(("w", "dmo", cnt["dmo"]))

    with nc.Block() as blk:
        def mk(ename):
            def body(eng):
                for item in ops[ename]:
                    if item[0] == "w":
                        eng.wait_ge(sems[item[1]], item[2])
                    elif item[0] == "o":
                        item[1](eng)
                    else:
                        item[1](eng).then_inc(sems[item[2]], item[3])
            return body
        blk.sync(mk("sync"))
        if ops["gpsimd"]:
            blk.gpsimd(mk("gpsimd"))
        blk.tensor(mk("tensor"))
        blk.vector(mk("vector"))
        blk.scalar(mk("scalar"))

    stack.close()
    return nc


_NC_CACHE = []


def kernel(L0_attn_x, L0_attn_W, L0_ffn_W_r, L0_ffn_W_k, L0_ffn_W_v, L0_ffn_mu_r, L0_ffn_mu_k):
    x = np.asarray(L0_attn_x, dtype=np.float32)
    wat = np.ascontiguousarray(np.asarray(L0_attn_W, np.float32).T)
    def _dckmaj(w):
        wt = np.asarray(w, np.float32).T.reshape(8, 128, 8, 128)
        return np.ascontiguousarray(wt.transpose(2, 1, 0, 3))

    wtr = _dckmaj(L0_ffn_W_r)
    wtk = _dckmaj(L0_ffn_W_k)
    wtv = _dckmaj(L0_ffn_W_v)
    mur = np.ascontiguousarray(np.asarray(L0_ffn_mu_r, np.float32).reshape(NCH, 128).T)
    muk = np.ascontiguousarray(np.asarray(L0_ffn_mu_k, np.float32).reshape(NCH, 128).T)
    ident = np.eye(128, dtype=np.float32)
    uu = np.arange(1024)[None, :]
    ii = np.arange(128)[:, None]
    mask = np.where(uu - 384 <= ii, np.float32(0.0), np.float32(-1e30)).astype(ml_dtypes.bfloat16)

    if not _NC_CACHE:
        _NC_CACHE.append(build_program())
    nc = _NC_CACHE[0]
    shared = {"wat": wat, "wtr": wtr, "wtk": wtk, "wtv": wtv, "mur": mur, "muk": muk,
              "identm": ident, "maskadd": mask}
    in_maps = [dict(shared, x0=np.ascontiguousarray(x[b].T)) for b in range(8)]
    res = run_bass_kernel_spmd(nc, in_maps, core_ids=list(range(8)))
    return np.stack([np.asarray(r["y"]).T for r in res.results], axis=0).astype(np.float32)
